# version 11
# baseline (speedup 1.0000x reference)
import numpy as np
import ml_dtypes
import concourse.bass as bass
import concourse.mybir as mybir
from concourse.bass_utils import run_bass_kernel_spmd

F32 = mybir.dt.float32
BF16 = mybir.dt.bfloat16
FP8 = mybir.dt.float8e4
AF = mybir.ActivationFunctionType
ALU = mybir.AluOpType
AX = mybir.AxisListType

NCORES = 8
D = 2048
E3 = 3 * D
II = 8192
ROT = 1024
HROT = 512
SEQ = 8192
EPS_LN = 1e-5
C_CLAMP = 448e-30
ET = 512
FIT = 512
DET = 512

DEBUG = False


def _rope_tables(T, core):
    j2 = np.arange(0, ROT, 2, dtype=np.float32)
    inv = (np.float32(1.0) / (np.float32(10000.0) ** (j2 / np.float32(ROT)))).astype(np.float32)
    pos = np.arange(core * T, (core + 1) * T, dtype=np.float32)
    fr = pos[:, None] * inv[None, :]
    return (np.cos(fr).astype(ml_dtypes.bfloat16),
            np.sin(fr).astype(ml_dtypes.bfloat16))


def build_nc(T=1024):
    SC = T // 128
    NH = 2 if SC >= 2 else 1
    SCH = SC // NH
    KCH = D // 128
    GSH = II // NCORES
    AG_ROWS = 2 * D + 2 * GSH
    ICH = II // 128
    NFIT = II // FIT
    NDET = D // DET
    HW = 128 * SCH
    BIGW = max(2 * SC * D, ICH * HW)
    UW = max(KCH * T, KCH * GSH)

    nc = bass.Bass("TRN2", target_bir_lowering=False, detect_race_conditions=False)

    x_in = nc.dram_tensor("x_in", [T, D], F32, kind="ExternalInput").ap()
    winT = nc.dram_tensor("winT", [D, E3], F32, kind="ExternalInput").ap()
    binx = nc.dram_tensor("binx", [1, E3], F32, kind="ExternalInput").ap()
    woutT = nc.dram_tensor("woutT", [D, D], F32, kind="ExternalInput").ap()
    boutx = nc.dram_tensor("boutx", [1, D], F32, kind="ExternalInput").ap()
    gsh = nc.dram_tensor("gsh", [GSH, D], F32, kind="ExternalInput").ap()
    ush = nc.dram_tensor("ush", [GSH, D], F32, kind="ExternalInput").ap()
    dsh = nc.dram_tensor("dsh", [D, GSH], F32, kind="ExternalInput").ap()
    cosx = nc.dram_tensor("cosx", [T, HROT], BF16, kind="ExternalInput").ap()
    sinx = nc.dram_tensor("sinx", [T, HROT], BF16, kind="ExternalInput").ap()
    identx = nc.dram_tensor("identx", [128, 128], F32, kind="ExternalInput").ap()
    out = nc.dram_tensor("out", [T, D], F32, kind="ExternalOutput").ap()

    ag_in_g = nc.dram_tensor("ag_in_g", [D, GSH], BF16)
    ag_in_u = nc.dram_tensor("ag_in_u", [D, GSH], BF16)
    ag_in_d = nc.dram_tensor("ag_in_d", [2 * GSH, GSH], BF16)
    ag_out_g = nc.dram_tensor("ag_out_g", [NCORES * D, GSH], BF16)
    ag_out_u = nc.dram_tensor("ag_out_u", [NCORES * D, GSH], BF16)
    ag_out_d = nc.dram_tensor("ag_out_d", [NCORES * 2 * GSH, GSH], BF16)
    ar_in = nc.dram_tensor("ar_in", [1, D], F32)
    ar_out = nc.dram_tensor("ar_out", [1, D], F32)
    x1sp = nc.dram_tensor("x1sp", [T, D], F32)

    dbg = {}
    if DEBUG:
        for nm, sh in [("dbg_a", [1, D]), ("dbg_x1", [T, D]), ("dbg_u", [T, D]),
                       ("dbg_ai", [T, D]), ("dbg_ut", [128, UW]), ("dbg_u0", [128, D]),
                       ("dbg_st", [128, 48]), ("dbg_x0", [128, D]),
                       ("dbg_rn", [128, 32]), ("dbg_tkv", [128, D]), ("dbg_u2", [128, D]),
                       ("dbg_l2", [128, 8])]:
            dbg[nm] = nc.dram_tensor(nm, sh, F32, kind="ExternalOutput").ap()

    from contextlib import ExitStack
    with ExitStack() as stack:
        sb = lambda n_, sh, dt: stack.enter_context(nc.sbuf_tensor(n_, sh, dt))
        ps = lambda n_, sh, dt: stack.enter_context(nc.psum_tensor(n_, sh, dt))
        sem = lambda n_: stack.enter_context(nc.semaphore(n_))

        uT = sb("uT", [128, UW], BF16)
        big = sb("big", [128, BIGW], BF16)
        aout = sb("aout", [128, max(SC * D, 16384)], BF16)
        wslX = sb("wslX", [128, 16384], BF16)
        dump = sb("dump", [128, D], F32)
        xch = sb("xch", [128, D], F32)
        tls = sb("tls", [128, D], F32)
        tabc = sb("tabc", [128, HROT], BF16)
        tabs = sb("tabs", [128, HROT], BF16)
        wdq = sb("wdq", [128, 6144], BF16)
        fh = [wdq[:, i * FIT:(i + 1) * FIT] for i in range(4)]
        qt = sb("qt", [128, D], FP8)
        bm = sb("bm", [128, 64], F32)
        rs2 = sb("rs2", [128, 64], F32)
        s2v = sb("s2v", [128, 64], F32)
        sums = sb("sums", [128, max(SC, 2)], F32)
        ssq = sb("ssq", [128, max(SC, 2)], F32)
        mvec = sb("mvec", [128, max(SC, 2)], F32)
        vvec = sb("vvec", [128, max(SC, 2)], F32)
        rstd = sb("rstd", [128, max(SC, 2)], F32)
        t2 = sb("t2", [128, max(SC, 2)], F32)
        rnb = sb("rnb", [128, max(SC, 2)], BF16)
        abc = sb("abc", [128, D], BF16)
        ab16 = abc[0:1, :]
        bsl = sb("bsl", [1, 2 * ET], BF16)
        epsv = sb("epsv", [128, 1], F32)
        ones1 = sb("ones1", [1, 128], BF16)
        identb = sb("identb", [128, 128], BF16)
        asb = dump[0:1, :]

        padt = sb("padt", [128, 128], BF16)
        winA = xch
        winB = tls

        pb = [ps(f"pb{i}", [128, 512], F32) for i in range(8)]
        ptr_b = pb[7][:, :].bitcast(BF16)
        ptr_b2 = pb[6][:, :].bitcast(BF16)

        s_ld = sem("s_ld")
        g_ld = sem("g_ld")
        g_ld2 = sem("g_ld2")
        s_st = sem("s_st")
        mm = sem("mm")
        ccW = sem("ccW")
        ccA = sem("ccA")

        C = {"s_ld": 0, "g_ld": 0, "g_ld2": 0, "s_st": 0, "mm": 0}

        def bump(k, n=1):
            C[k] += n
            return C[k]

        def pad(v):
            v.tensor_copy(padt[:, 0:64], padt[:, 64:128])
            v.tensor_copy(padt[:, 0:64], padt[:, 64:128])

        def emit_qdq(v, src_ap, dst_ap, W):
            nb = W // 32
            v.reduce_max(bm[:, 0:nb], src_ap.rearrange("p (b k) -> p b k", k=32),
                         axis=AX.X, apply_absolute_value=True)
            pad(v)
            v.tensor_scalar_max(bm[:, 0:nb], bm[:, 0:nb], C_CLAMP)
            pad(v)
            v.reciprocal(rs2[:, 0:nb], bm[:, 0:nb])
            v.tensor_scalar_mul(s2v[:, 0:nb], bm[:, 0:nb], 1.0 / 224.0)
            v.tensor_scalar_mul(rs2[:, 0:nb], rs2[:, 0:nb], 224.0)
            pad(v)
            rse = rs2[:, 0:nb].unsqueeze(2).broadcast_to([128, nb, 32])
            s2e = s2v[:, 0:nb].unsqueeze(2).broadcast_to([128, nb, 32])
            v.tensor_tensor(qt[:, 0:W].rearrange("p (b k) -> p b k", k=32),
                            src_ap.rearrange("p (b k) -> p b k", k=32), rse, op=ALU.mult)
            v.tensor_tensor(dst_ap.rearrange("p (b k) -> p b k", k=32),
                            qt[:, 0:W].rearrange("p (b k) -> p b k", k=32), s2e,
                            op=ALU.mult)

        def tr_blocks(n128, src_fn, dst_fn):
            for b0 in range(0, (n128 + 7) // 8, 2):
                nb = min(2, (n128 + 7) // 8 - b0)
                mvs = [bump("mm") for _ in range(nb)]
                with nc.Block() as blk:
                    @blk.tensor
                    def _(t, b0=b0, nb=nb):
                        for bb in range(nb):
                            ptr = ptr_b if bb == 0 else ptr_b2
                            k0 = (b0 + bb) * 8
                            kn = min(8, n128 - k0)
                            for a in range(kn):
                                last = t.transpose(ptr[:, a * 128:(a + 1) * 128],
                                                   src_fn(k0 + a), identb[:, :])
                            last.then_inc(mm, 1)

                    @blk.vector
                    def _(v, b0=b0, nb=nb, mvs=mvs):
                        for bb in range(nb):
                            ptr = ptr_b if bb == 0 else ptr_b2
                            k0 = (b0 + bb) * 8
                            kn = min(8, n128 - k0)
                            v.wait_ge(mm, mvs[bb])
                            for a in range(kn):
                                v.tensor_copy(dst_fn(k0 + a), ptr[:, a * 128:(a + 1) * 128])

        def ln_stats(src_ap, j, out_u=None):
            with nc.Block() as blk:
                @blk.vector
                def _(v, j=j):
                    v.reduce_sum(sums[:, j:j + 1], src_ap, axis=AX.X)

                @blk.scalar
                def _(s, j=j):
                    s.activation(dump[:, :], src_ap, AF.Square)
            with nc.Block() as blk:
                @blk.vector
                def _(v, j=j):
                    v.reduce_sum(ssq[:, j:j + 1], dump[:, :], axis=AX.X)
                    v.tensor_scalar_mul(mvec[:, j:j + 1], sums[:, j:j + 1], 1.0 / D)
                    pad(v)
                    v.tensor_scalar_mul(vvec[:, j:j + 1], ssq[:, j:j + 1], 1.0 / D)
                    v.tensor_tensor(t2[:, j:j + 1], mvec[:, j:j + 1], mvec[:, j:j + 1],
                                    op=ALU.mult)
                    pad(v)
                    v.tensor_tensor(vvec[:, j:j + 1], vvec[:, j:j + 1], t2[:, j:j + 1],
                                    op=ALU.subtract)

                @blk.scalar
                def _(s):
                    pass
            with nc.Block() as blk:
                @blk.scalar
                def _(s, j=j):
                    s.activation(t2[:, j:j + 1], vvec[:, j:j + 1], AF.Sqrt,
                                 bias=epsv[:, 0:1])

                @blk.vector
                def _(v):
                    pass
            if out_u is not None:
                with nc.Block() as blk:
                    @blk.vector
                    def _(v, j=j):
                        v.reciprocal(rstd[:, j:j + 1], t2[:, j:j + 1])
                        pad(v)
                        v.tensor_scalar(out_u, src_ap, mvec[:, j:j + 1], rstd[:, j:j + 1],
                                        op0=ALU.subtract, op1=ALU.mult)

                    @blk.scalar
                    def _(s):
                        pass

        NWCH = 32

        def wchunk(i):
            if i < 8:
                return "g", i
            if i < 16:
                return "u", i - 8
            return "d", i - 16

        wload_marks = [None] * NWCH

        def w_load(sy, i):
            if i < 0 or i >= NWCH:
                return
            kind, p = wchunk(i)
            src = {"g": gsh, "u": ush, "d": dsh}[kind]
            W = GSH if kind == "d" else D
            buf = winA if i % 2 == 0 else winB
            wload_marks[i] = bump("s_ld", 16)
            sy.dma_start(out=buf[:, 0:W],
                         in_=src[p * 128:(p + 1) * 128, 0:W]).then_inc(s_ld, 16)

        def w_qdq(v, i):
            if i < 0 or i >= NWCH:
                return
            kind, p = wchunk(i)
            W = GSH if kind == "d" else D
            buf = winA if i % 2 == 0 else winB
            q = i % 3
            v.wait_ge(s_ld, wload_marks[i])
            emit_qdq(v, buf[:, 0:W], wdq[:, q * 2048: q * 2048 + W], W)

        def w_ntrb(i):
            if i < 0 or i >= NWCH:
                return 0
            return 1 if wchunk(i)[0] == "d" else 2

        def w_tr_emit(t, i):
            if i < 0 or i >= NWCH:
                return
            q = i % 3
            for b in range(w_ntrb(i)):
                ptr = ptr_b if b == 0 else ptr_b2
                for a in range(8):
                    k = b * 8 + a
                    last = t.transpose(
                        ptr[:, a * 128:(a + 1) * 128],
                        wdq[:, q * 2048 + k * 128: q * 2048 + (k + 1) * 128],
                        identb[:, :])
                last.then_inc(mm, 1)

        def w_tr_evac(v, i, trm):
            if i < 0 or i >= NWCH:
                return
            q = i % 3
            for b in range(len(trm)):
                ptr = ptr_b if b == 0 else ptr_b2
                v.wait_ge(mm, trm[b])
                for a in range(8):
                    k = b * 8 + a
                    v.tensor_copy(wdq[:, q * 2048 + k * 128: q * 2048 + (k + 1) * 128],
                                  ptr[:, a * 128:(a + 1) * 128])

        def w_stage(sy, i):
            if i < 0 or i >= NWCH:
                return
            kind, p = wchunk(i)
            q = i % 3
            if kind == "d":
                mark = bump("s_st", 32)
                for half in range(2):
                    dst = ag_in_d.ap() \
                        .rearrange("(i two) g -> i two g", two=2)[
                            half * 512:(half + 1) * 512, p // 8:p // 8 + 1,
                            (p % 8) * 128:(p % 8) * 128 + 128] \
                        .rearrange("(c ip) one r -> ip (c one) r", ip=128)
                    sy.dma_start(
                        out=dst,
                        in_=wdq[:, q * 2048 + half * 512: q * 2048 + (half + 1) * 512]
                        .rearrange("ip (c r) -> ip c r", r=128)).then_inc(s_st, 16)
            else:
                mark = bump("s_st", 64)
                ag = ag_in_g if kind == "g" else ag_in_u
                for qq in range(4):
                    dst = ag.ap()[qq * 512:(qq + 1) * 512, p * 128:(p + 1) * 128] \
                        .rearrange("(k r) c -> r k c", r=128)
                    sy.dma_start(
                        out=dst,
                        in_=wdq[:, q * 2048 + qq * 512: q * 2048 + (qq + 1) * 512]
                        .rearrange("r (k c) -> r k c", c=128)).then_inc(s_st, 16)
            sy.wait_ge(s_st, mark)

        def w_kick(ag_i, ag_o):
            def f(g):
                g.collective_compute(
                    "AllGather", ALU.bypass, replica_groups=[list(range(NCORES))],
                    ins=[ag_i.ap().opt()], outs=[ag_o.ap().opt()],
                ).then_inc(ccW, 1)
            return f

        WS = [0]

        with nc.Block() as blk:
            @blk.gpsimd
            def _(g):
                g.dma_start(out=identb[:, :], in_=identx).then_inc(g_ld, 16)
                bump("g_ld", 16)
                g.wait_ge(g_ld, C["g_ld"])
                g.memset(ones1[:, :], 1.0)
                g.memset(epsv[:, :], EPS_LN)
                g.memset(padt[:, :], 0.0)

            @blk.vector
            def _(v):
                pass

        UO = 0
        KQ = SC * D
        for j in range(SC):
            ld_here = bump("s_ld", 16)
            with nc.Block() as blk:
                @blk.sync
                def _(sy, j=j):
                    sy.dma_start(out=xch[:, :],
                                 in_=x_in[j * 128:(j + 1) * 128, :]).then_inc(s_ld, 16)

                @blk.vector
                def _(v, j=j, ld_here=ld_here):
                    v.wait_ge(s_ld, ld_here)
                    v.tensor_tensor(dump[:, :], xch[:, :], xch[:, :], op=ALU.mult)
                    v.reduce_sum(sums[:, j:j + 1], xch[:, :], axis=AX.X)
                    v.reduce_sum(ssq[:, j:j + 1], dump[:, :], axis=AX.X)
                    v.tensor_copy(big[:, UO + j * D: UO + (j + 1) * D], xch[:, :])
        with nc.Block() as blk:
            @blk.vector
            def _(v):
                for j in range(SC):
                    v.tensor_scalar_mul(mvec[:, j:j + 1], sums[:, j:j + 1], 1.0 / D)
                for j in range(SC):
                    v.tensor_scalar_mul(vvec[:, j:j + 1], ssq[:, j:j + 1], 1.0 / D)
                pad(v)
                for j in range(SC):
                    v.tensor_tensor(t2[:, j:j + 1], mvec[:, j:j + 1], mvec[:, j:j + 1],
                                    op=ALU.mult)
                pad(v)
                for j in range(SC):
                    v.tensor_tensor(vvec[:, j:j + 1], vvec[:, j:j + 1], t2[:, j:j + 1],
                                    op=ALU.subtract)

            @blk.scalar
            def _(s):
                pass
        with nc.Block() as blk:
            @blk.scalar
            def _(s):
                s.activation(t2[:, 0:SC], vvec[:, 0:SC], AF.Sqrt, bias=epsv[:, 0:1])

            @blk.vector
            def _(v):
                pass
        with nc.Block() as blk:
            @blk.vector
            def _(v):
                v.reciprocal(rstd[:, 0:SC], t2[:, 0:SC])
                pad(v)
                for j in range(SC):
                    v.tensor_scalar(big[:, UO + j * D: UO + (j + 1) * D],
                                    big[:, UO + j * D: UO + (j + 1) * D],
                                    mvec[:, j:j + 1], rstd[:, j:j + 1],
                                    op0=ALU.subtract, op1=ALU.mult)

            @blk.scalar
            def _(s):
                pass
        for j in range(SC):
            tr_blocks(KCH,
                      lambda k, j=j: big[:, UO + j * D + k * 128: UO + j * D + (k + 1) * 128],
                      lambda k, j=j: uT[:, k * T + j * 128: k * T + (j + 1) * 128])

        def qkv_load_plan(ti):
            et = ti if ti < 12 else ti - 12
            use_wout = ti >= 12
            buf = ti % 2
            g_here = bump("g_ld", 80)
            wsrc = woutT if use_wout else winT
            bsrc = boutx if use_wout else binx

            def emit(g, et=et, buf=buf, wsrc=wsrc, bsrc=bsrc):
                for q in range(4):
                    src3 = wsrc[q * 512:(q + 1) * 512, et * ET:(et + 1) * ET] \
                        .rearrange("(k r) e -> r k e", r=128)
                    dst3 = wslX[:, buf * 8192 + q * 4 * ET: buf * 8192 + (q + 1) * 4 * ET] \
                        .rearrange("r (k e) -> r k e", k=4)
                    g.dma_start(out=dst3, in_=src3).then_inc(g_ld, 16)
                g.dma_start(out=bsl[:, buf * ET:(buf + 1) * ET],
                            in_=bsrc[0:1, et * ET:(et + 1) * ET]).then_inc(g_ld, 16)
            return g_here, emit

        qkv_loads = [qkv_load_plan(ti) for ti in range(16)]

        def qkv_etile(ti, dst_desc, lhs_tensor=None, emit_next=True, gext=None):
            lhs_tensor = uT if lhs_tensor is None else lhs_tensor
            buf = ti % 2
            g_need = qkv_loads[ti][0]
            load_emit = qkv_loads[ti + 1][1] if (emit_next and ti + 1 < 16) else None
            for half, jlist_ in ((0, list(range(min(SC, 6)))), (1, list(range(6, SC)))):
                ci = WS[0]
                WS[0] += 1
                mmv = [bump("mm") for _ in jlist_]
                trm = [bump("mm") for _ in range(w_ntrb(ci - 1))]
                with nc.Block() as blk:
                    @blk.gpsimd
                    def _(g, half=half):
                        if half == 0 and load_emit is not None:
                            load_emit(g)
                        if half == 0 and gext is not None:
                            gext(g)

                    @blk.sync
                    def _(sy, ci=ci):
                        w_stage(sy, ci - 2)
                        w_load(sy, ci + 1)

                    @blk.tensor
                    def _(t, buf=buf, g_need=g_need, half=half, jlist_=jlist_, ci=ci):
                        if half == 0:
                            t.wait_ge(g_ld, g_need)
                        for j in jlist_:
                            bank = pb[j % 6]
                            for k in range(KCH):
                                t.matmul(bank[:, 0:ET],
                                         lhs_tensor[:, k * T + j * 128: k * T + (j + 1) * 128],
                                         wslX[:, buf * 8192 + k * ET: buf * 8192 + (k + 1) * ET],
                                         start=(k == 0), stop=False)
                            last = t.matmul(bank[:, 0:ET], ones1[:, :],
                                            bsl[0:1, buf * ET:(buf + 1) * ET],
                                            start=False, stop=True)
                            last.then_inc(mm, 1)
                        w_tr_emit(t, ci - 1)

                    @blk.vector
                    def _(v, jlist_=jlist_, mmv=mmv, trm=trm, ci=ci):
                        for jj, j in enumerate(jlist_):
                            v.wait_ge(mm, mmv[jj])
                            dst_desc(v, j, pb[j % 6])
                        w_qdq(v, ci)
                        w_tr_evac(v, ci - 1, trm)

        def evac_deint(dst, dbase, toff):
            def f(v, j, bank):
                src2 = bank[:, 0:ET].rearrange("p (a two) -> p two a", two=2)
                v.tensor_copy(dst[:, dbase + j * D + toff * 256:
                                  dbase + j * D + toff * 256 + 256], src2[:, 0, :])
                v.tensor_copy(dst[:, dbase + j * D + HROT + toff * 256:
                                  dbase + j * D + HROT + toff * 256 + 256], src2[:, 1, :])
            return f

        def evac_plain(dst, dbase, coloff):
            def f(v, j, bank):
                v.tensor_copy(dst[:, dbase + j * D + coloff: dbase + j * D + coloff + ET],
                              bank[:, 0:ET])
            return f

        def rope_pass(region_base, scr):
            for j in range(SC):
                ld_here = bump("s_ld", 32)
                with nc.Block() as blk:
                    @blk.sync
                    def _(sy, j=j):
                        sy.dma_start(out=tabc[:, :],
                                     in_=cosx[j * 128:(j + 1) * 128, :]).then_inc(s_ld, 16)
                        sy.dma_start(out=tabs[:, :],
                                     in_=sinx[j * 128:(j + 1) * 128, :]).then_inc(s_ld, 16)

                    @blk.vector
                    def _(v, j=j, ld_here=ld_here):
                        v.wait_ge(s_ld, ld_here)
                        e_ap = big[:, region_base + j * D: region_base + j * D + HROT]
                        o_ap = big[:, region_base + j * D + HROT:
                                   region_base + j * D + 2 * HROT]
                        rA = wdq[:, scr: scr + HROT]
                        rB = wdq[:, scr + HROT: scr + 2 * HROT]
                        rC = wdq[:, scr + 2 * HROT: scr + 3 * HROT]
                        rD = wdq[:, scr + 3 * HROT: scr + 4 * HROT]
                        v.tensor_tensor(rA, e_ap, tabc[:, :], op=ALU.mult)
                        v.tensor_tensor(rB, o_ap, tabs[:, :], op=ALU.mult)
                        v.tensor_tensor(rC, o_ap, tabc[:, :], op=ALU.mult)
                        v.tensor_tensor(rD, e_ap, tabs[:, :], op=ALU.mult)
                        v.tensor_tensor(e_ap, rA, rB, op=ALU.subtract)
                        v.tensor_tensor(o_ap, rC, rD, op=ALU.add)

        with nc.Block() as blk:
            @blk.gpsimd
            def _(g):
                qkv_loads[0][1](g)

            @blk.sync
            def _(sy):
                w_load(sy, 0)

            @blk.vector
            def _(v):
                pass

        for ti in range(0, 2):
            qkv_etile(ti, evac_deint(big, KQ, ti))
        for ti in range(2, 4):
            qkv_etile(ti, evac_plain(big, KQ, ROT + (ti - 2) * ET))
        rope_pass(KQ, 2 * 2048)

        for j in range(SC):
            with nc.Block() as blk:
                @blk.scalar
                def _(s, j=j):
                    s.activation(dump[:, :], big[:, KQ + j * D: KQ + (j + 1) * D],
                                 AF.Square)

                @blk.vector
                def _(v):
                    pass
            with nc.Block() as blk:
                @blk.vector
                def _(v, j=j):
                    v.reduce_sum(ssq[:, j:j + 1], dump[:, :], axis=AX.X)

                @blk.scalar
                def _(s):
                    pass
        with nc.Block() as blk:
            @blk.scalar
            def _(s):
                s.activation(sums[:, 0:SC], ssq[:, 0:SC], AF.Sqrt)

            @blk.vector
            def _(v):
                pass
        with nc.Block() as blk:
            @blk.vector
            def _(v):
                v.tensor_scalar_max(sums[:, 0:SC], sums[:, 0:SC], 1e-12)
                pad(v)
                v.reciprocal(t2[:, 0:SC], sums[:, 0:SC])
                pad(v)
                v.tensor_copy(rnb[:, 0:SC], t2[:, 0:SC])

            @blk.scalar
            def _(s):
                pass
        if DEBUG:
            with nc.Block() as blk:
                @blk.vector
                def _(v):
                    v.tensor_copy(dump[:, 0:SC], rnb[:, 0:SC])

                @blk.scalar
                def _(s):
                    pass
            gd = bump("g_ld2", 64)
            with nc.Block() as blk:
                @blk.gpsimd
                def _(g, gd=gd):
                    g.dma_start(out=dbg["dbg_rn"][:, 0:8], in_=ssq[:, 0:SC]).then_inc(g_ld2, 16)
                    g.dma_start(out=dbg["dbg_rn"][:, 8:16], in_=sums[:, 0:SC]).then_inc(g_ld2, 16)
                    g.dma_start(out=dbg["dbg_rn"][:, 16:24], in_=t2[:, 0:SC]).then_inc(g_ld2, 16)
                    g.dma_start(out=dbg["dbg_rn"][:, 24:32], in_=dump[:, 0:SC]).then_inc(g_ld2, 16)
                    g.wait_ge(g_ld2, gd)

                @blk.vector
                def _(v):
                    pass

        qkv_etile(4, evac_deint(aout, 0, 0))
        qkv_etile(5, evac_deint(aout, 0, 1))
        for ti in range(6, 8):
            qkv_etile(ti, evac_plain(aout, 0, ROT + (ti - 6) * ET))

        for j in range(SC):
            with nc.Block() as blk:
                @blk.vector
                def _(v, j=j):
                    v.tensor_tensor(aout[:, j * D:(j + 1) * D],
                                    big[:, KQ + j * D: KQ + (j + 1) * D],
                                    aout[:, j * D:(j + 1) * D], op=ALU.mult)

                @blk.tensor
                def _(t):
                    pass
        if DEBUG:
            gd = bump("g_ld2", 16)
            with nc.Block() as blk:
                @blk.gpsimd
                def _(g, gd=gd):
                    g.dma_start(out=dbg["dbg_tkv"], in_=aout[:, 0:D]).then_inc(g_ld2, 16)
                    g.wait_ge(g_ld2, gd)

                @blk.vector
                def _(v):
                    pass
        for vs in range(4):
            mv = bump("mm")
            with nc.Block() as blk:
                @blk.tensor
                def _(t, vs=vs):
                    for j in range(SC):
                        last = t.matmul(pb[6][0:1, :], rnb[:, j:j + 1],
                                        aout[:, j * D + vs * 512: j * D + (vs + 1) * 512],
                                        start=(j == 0), stop=(j == SC - 1))
                    last.then_inc(mm, 1)

                @blk.vector
                def _(v, vs=vs, mv=mv):
                    v.wait_ge(mm, mv)
                    v.tensor_copy(asb[0:1, vs * 512:(vs + 1) * 512], pb[6][0:1, :])

        g1 = bump("g_ld2", 16)
        with nc.Block() as blk:
            @blk.gpsimd
            def _(g, g1=g1):
                g.dma_start(out=ar_in.ap(), in_=asb[:, :]).then_inc(g_ld2, 16)
                g.wait_ge(g_ld2, g1)
                g.collective_compute(
                    "AllReduce", ALU.add, replica_groups=[list(range(NCORES))],
                    ins=[ar_in.ap().opt()], outs=[ar_out.ap().opt()],
                ).then_inc(ccA, 1)

            @blk.vector
            def _(v):
                pass

        qkv_etile(8, evac_deint(big, KQ, 0), gext=w_kick(ag_in_g, ag_out_g))
        qkv_etile(9, evac_deint(big, KQ, 1), gext=w_kick(ag_in_u, ag_out_u))
        for ti in range(10, 12):
            qkv_etile(ti, evac_plain(big, KQ, ROT + (ti - 10) * ET))
        rope_pass(KQ, 0)

        g2 = bump("g_ld2", 16)
        with nc.Block() as blk:
            @blk.gpsimd
            def _(g):
                g.wait_ge(ccA, 1)
                g.dma_start(out=asb[:, :], in_=ar_out.ap()).then_inc(g_ld2, 16)

            @blk.vector
            def _(v, g2=g2):
                v.wait_ge(g_ld2, g2)
                v.tensor_copy(ab16[:, :], asb[:, :])
        mm4 = [bump("mm") for _ in range(4)]
        with nc.Block() as blk:
            @blk.tensor
            def _(t):
                for q in range(4):
                    t.matmul(pb[q][:, :], ones1[:, :], ab16[0:1, q * 512:(q + 1) * 512],
                             start=True, stop=True).then_inc(mm, 1)

            @blk.vector
            def _(v, mm4=mm4):
                for q in range(4):
                    v.wait_ge(mm, mm4[q])
                    v.tensor_copy(abc[:, q * 512:(q + 1) * 512], pb[q][:, :])

        if DEBUG:
            st_here = bump("s_st", 16)
            with nc.Block() as blk:
                @blk.sync
                def _(sy, st_here=st_here):
                    sy.dma_start(out=dbg["dbg_a"], in_=asb[:, :]).then_inc(s_st, 16)
                    sy.wait_ge(s_st, st_here)

                @blk.vector
                def _(v):
                    pass

        for j in range(SC):
            with nc.Block() as blk:
                @blk.vector
                def _(v, j=j):
                    v.tensor_tensor(big[:, KQ + j * D: KQ + (j + 1) * D],
                                    big[:, KQ + j * D: KQ + (j + 1) * D],
                                    abc[:, :], op=ALU.mult)

                @blk.tensor
                def _(t):
                    pass
            tr_blocks(KCH,
                      lambda k, j=j: big[:, KQ + j * D + k * 128: KQ + j * D + (k + 1) * 128],
                      lambda k, j=j: uT[:, k * T + j * 128: k * T + (j + 1) * 128])

        def evac_outproj(et):
            def f(v, j, bank):
                v.tensor_tensor(aout[:, j * D + et * ET: j * D + (et + 1) * ET],
                                bank[:, 0:ET],
                                big[:, UO + j * D + et * ET: UO + j * D + (et + 1) * ET],
                                op=ALU.add)
            return f

        for ti in range(12, 16):
            qkv_etile(ti, evac_outproj(ti - 12), lhs_tensor=uT)

        trm_d = [bump("mm") for _ in range(w_ntrb(NWCH - 1))]
        with nc.Block() as blk:
            @blk.tensor
            def _(t):
                w_tr_emit(t, NWCH - 1)

            @blk.sync
            def _(sy):
                w_stage(sy, NWCH - 2)

            @blk.vector
            def _(v, trm_d=trm_d):
                w_tr_evac(v, NWCH - 1, trm_d)
        with nc.Block() as blk:
            @blk.sync
            def _(sy):
                w_stage(sy, NWCH - 1)

            @blk.vector
            def _(v):
                pass
        with nc.Block() as blk:
            @blk.gpsimd
            def _(g):
                w_kick(ag_in_d, ag_out_d)(g)

            @blk.vector
            def _(v):
                pass

        if DEBUG:
            gd = bump("g_ld2", 16 * SC)
            with nc.Block() as blk:
                @blk.gpsimd
                def _(g, gd=gd):
                    for j in range(SC):
                        g.dma_start(out=dbg["dbg_ai"][j * 128:(j + 1) * 128, :],
                                    in_=aout[:, j * D:(j + 1) * D]).then_inc(g_ld2, 16)
                    g.wait_ge(g_ld2, gd)

                @blk.vector
                def _(v):
                    pass

        with nc.Block() as blk:
            @blk.vector
            def _(v):
                for j in range(SC):
                    v.tensor_tensor(dump[:, :], aout[:, j * D:(j + 1) * D],
                                    aout[:, j * D:(j + 1) * D], op=ALU.mult)
                    v.reduce_sum(sums[:, j:j + 1], aout[:, j * D:(j + 1) * D], axis=AX.X)
                    v.reduce_sum(ssq[:, j:j + 1], dump[:, :], axis=AX.X)

            @blk.scalar
            def _(s):
                pass

        def ln_math_block():
            with nc.Block() as blk:
                @blk.vector
                def _(v):
                    for j in range(SC):
                        v.tensor_scalar_mul(mvec[:, j:j + 1], sums[:, j:j + 1], 1.0 / D)
                    for j in range(SC):
                        v.tensor_scalar_mul(vvec[:, j:j + 1], ssq[:, j:j + 1], 1.0 / D)
                    pad(v)
                    for j in range(SC):
                        v.tensor_tensor(t2[:, j:j + 1], mvec[:, j:j + 1], mvec[:, j:j + 1],
                                        op=ALU.mult)
                    pad(v)
                    for j in range(SC):
                        v.tensor_tensor(vvec[:, j:j + 1], vvec[:, j:j + 1], t2[:, j:j + 1],
                                        op=ALU.subtract)

                @blk.scalar
                def _(s):
                    pass
            with nc.Block() as blk:
                @blk.scalar
                def _(s):
                    s.activation(t2[:, 0:SC], vvec[:, 0:SC], AF.Sqrt, bias=epsv[:, 0:1])

                @blk.vector
                def _(v):
                    pass
            with nc.Block() as blk:
                @blk.vector
                def _(v):
                    v.reciprocal(rstd[:, 0:SC], t2[:, 0:SC])
                    pad(v)

                @blk.scalar
                def _(s):
                    pass

        ln_math_block()
        with nc.Block() as blk:
            @blk.vector
            def _(v):
                for j in range(SC):
                    v.tensor_scalar(aout[:, j * D:(j + 1) * D], aout[:, j * D:(j + 1) * D],
                                    mvec[:, j:j + 1], rstd[:, j:j + 1],
                                    op0=ALU.subtract, op1=ALU.mult)

            @blk.scalar
            def _(s):
                pass
        for j in range(SC):
            ld_here = bump("s_ld", 16)
            with nc.Block() as blk:
                @blk.sync
                def _(sy, j=j):
                    sy.dma_start(out=xch[:, :],
                                 in_=x_in[j * 128:(j + 1) * 128, :]).then_inc(s_ld, 16)

                @blk.vector
                def _(v, j=j, ld_here=ld_here):
                    v.wait_ge(s_ld, ld_here)
                    v.tensor_tensor(tls[:, :], aout[:, j * D:(j + 1) * D], xch[:, :],
                                    op=ALU.add)
                    v.tensor_tensor(dump[:, :], tls[:, :], tls[:, :], op=ALU.mult)
                    v.reduce_sum(sums[:, j:j + 1], tls[:, :], axis=AX.X)
                    v.reduce_sum(ssq[:, j:j + 1], dump[:, :], axis=AX.X)
            st_here = bump("s_st", 16)
            with nc.Block() as blk:
                @blk.sync
                def _(sy, j=j, st_here=st_here):
                    sy.dma_start(out=x1sp.ap()[j * 128:(j + 1) * 128, :],
                                 in_=tls[:, :]).then_inc(s_st, 16)
                    sy.wait_ge(s_st, st_here)

                @blk.vector
                def _(v):
                    pass
        ln_math_block()
        for j in range(SC):
            ld_here = bump("s_ld", 16)
            with nc.Block() as blk:
                @blk.sync
                def _(sy, j=j):
                    sy.dma_start(out=xch[:, :],
                                 in_=x1sp.ap()[j * 128:(j + 1) * 128, :]).then_inc(s_ld, 16)

                @blk.vector
                def _(v, j=j, ld_here=ld_here):
                    v.wait_ge(s_ld, ld_here)
                    v.tensor_scalar(wslX[:, 0:D], xch[:, :], mvec[:, j:j + 1],
                                    rstd[:, j:j + 1], op0=ALU.subtract, op1=ALU.mult)
                    emit_qdq(v, wslX[:, 0:D], wslX[:, 0:D], D)
            tr_blocks(KCH,
                      lambda k: wslX[:, k * 128:(k + 1) * 128],
                      lambda k, j=j: uT[:, k * T + j * 128: k * T + (j + 1) * 128])

        for h in range(NH):
            jlist = list(range(h * SCH, (h + 1) * SCH))

            def gu_load_emit(it, gi):
                def emit(sy, it=it, gi=gi):
                    i0 = it * FIT
                    r = i0 // GSH
                    iL = i0 - r * GSH
                    agsrc = ag_out_g if gi == 0 else ag_out_u
                    for q in range(4):
                        src = agsrc.ap()[r * D + q * 512:
                                         r * D + (q + 1) * 512, iL:iL + FIT] \
                            .rearrange("(k p) i -> p k i", p=128)
                        sy.dma_start(out=wslX[:, gi * 8192 + q * 2048:
                                              gi * 8192 + (q + 1) * 2048]
                                     .rearrange("p (k i) -> p k i", k=4),
                                     in_=src).then_inc(s_ld, 16)
                return emit

            g0 = bump("s_ld", 64)
            u0 = bump("s_ld", 64)
            with nc.Block() as blk:
                @blk.sync
                def _(sy, h=h):
                    if h == 0:
                        sy.wait_ge(ccW, 2)
                    gu_load_emit(0, 0)(sy)
                    gu_load_emit(0, 1)(sy)

                @blk.vector
                def _(v):
                    pass
            FS = SC * D // 2 if SC * D // 2 >= 8192 else 8192
            FS = 8192

            def fsil_ap(par, jj):
                return aout[:, FS + par * 4096 + jj * FIT: FS + par * 4096 + (jj + 1) * FIT]

            def fup_ap(par, jj):
                return aout[:, FS + par * 4096 + 2048 + jj * FIT:
                            FS + par * 4096 + 2048 + (jj + 1) * FIT]

            def fhp(it, jj):
                return wdq[:, (it % 2) * 2048 + jj * FIT:
                           (it % 2) * 2048 + (jj + 1) * FIT]

            nuA = min(SCH, 2)
            nuB = SCH - nuA
            ld_marks = [(g0, u0)]
            for t in range(NFIT + 2):
                fills = t < NFIT
                it1 = t - 1
                it2 = t - 2
                ntrb = (SCH * (FIT // 128) + 7) // 8 if 0 <= it2 < NFIT else 0
                mm_gate = [bump("mm") for _ in range(SCH)] if fills else []
                mm_upA = [bump("mm") for _ in range(nuA)] if fills else []
                mm_tr = [bump("mm") for _ in range(ntrb)]
                gmark = bump("s_ld", 64) if t + 1 < NFIT else None
                with nc.Block() as blk:
                    @blk.tensor
                    def _(t_, t=t, it2=it2, fills=fills, jlist=jlist):
                        if fills:
                            t_.wait_ge(s_ld, ld_marks[t][0])
                            for jj, j in enumerate(jlist):
                                for k in range(KCH):
                                    last = t_.matmul(
                                        pb[jj][:, :],
                                        uT[:, k * T + j * 128: k * T + (j + 1) * 128],
                                        wslX[:, k * FIT:(k + 1) * FIT],
                                        start=(k == 0), stop=(k == KCH - 1))
                                last.then_inc(mm, 1)
                            t_.wait_ge(s_ld, ld_marks[t][1])
                            for jj in range(nuA):
                                j = jlist[jj]
                                for k in range(KCH):
                                    last = t_.matmul(
                                        pb[4 + jj][:, :],
                                        uT[:, k * T + j * 128: k * T + (j + 1) * 128],
                                        wslX[:, 8192 + k * FIT: 8192 + (k + 1) * FIT],
                                        start=(k == 0), stop=(k == KCH - 1))
                                last.then_inc(mm, 1)
                        if 0 <= it2 < NFIT:
                            for b in range((SCH * (FIT // 128) + 7) // 8):
                                ptr = ptr_b if b == 0 else ptr_b2
                                for a in range(min(8, SCH * (FIT // 128) - b * 8)):
                                    jj, kk = divmod(b * 8 + a, FIT // 128)
                                    last = t_.transpose(
                                        ptr[:, a * 128:(a + 1) * 128],
                                        fhp(it2, jj)[:, kk * 128:(kk + 1) * 128],
                                        identb[:, :])
                                last.then_inc(mm, 1)

                    @blk.scalar
                    def _(s_, t=t, fills=fills, mm_gate=mm_gate, mm_upA=mm_upA):
                        if not fills:
                            return
                        par = t % 2
                        for jj in range(SCH):
                            s_.wait_ge(mm, mm_gate[jj])
                            s_.activation(fsil_ap(par, jj), pb[jj][:, :], AF.Silu)
                        for jj in range(nuA):
                            s_.wait_ge(mm, mm_upA[jj])
                            s_.copy(fup_ap(par, jj), pb[4 + jj][:, :])

                    @blk.sync
                    def _(sy, t=t, mm_gate=mm_gate):
                        if t + 1 < NFIT:
                            sy.wait_ge(mm, mm_gate[SCH - 1])
                            gu_load_emit(t + 1, 0)(sy)

                    @blk.vector
                    def _(v, t=t, it1=it1, it2=it2, mm_tr=mm_tr):
                        if 0 <= it1 < NFIT:
                            par = it1 % 2
                            for jj in range(SCH):
                                v.tensor_tensor(fhp(it1, jj), fsil_ap(par, jj),
                                                fup_ap(par, jj), op=ALU.mult)
                                emit_qdq(v, fhp(it1, jj), fhp(it1, jj), FIT)
                        if 0 <= it2 < NFIT:
                            for b in range(len(mm_tr)):
                                ptr = ptr_b if b == 0 else ptr_b2
                                v.wait_ge(mm, mm_tr[b])
                                for a in range(min(8, SCH * (FIT // 128) - b * 8)):
                                    jj, kk = divmod(b * 8 + a, FIT // 128)
                                    v.tensor_copy(
                                        big[:, (it2 * (FIT // 128) + kk) * HW
                                            + jj * 128:
                                            (it2 * (FIT // 128) + kk) * HW
                                            + (jj + 1) * 128],
                                        ptr[:, a * 128:(a + 1) * 128])
                mm_upB = [bump("mm") for _ in range(nuB)] if fills else []
                umark = bump("s_ld", 64) if t + 1 < NFIT else None
                if (fills and nuB) or t + 1 < NFIT:
                    with nc.Block() as blk:
                        @blk.tensor
                        def _(t_, t=t, fills=fills, jlist=jlist):
                            if not fills:
                                return
                            for b in range(nuB):
                                jj = nuA + b
                                j = jlist[jj]
                                for k in range(KCH):
                                    last = t_.matmul(
                                        pb[4 + b][:, :],
                                        uT[:, k * T + j * 128: k * T + (j + 1) * 128],
                                        wslX[:, 8192 + k * FIT: 8192 + (k + 1) * FIT],
                                        start=(k == 0), stop=(k == KCH - 1))
                                last.then_inc(mm, 1)

                        @blk.scalar
                        def _(s_, t=t, fills=fills, mm_upB=mm_upB):
                            if not fills:
                                return
                            par = t % 2
                            for b in range(nuB):
                                s_.wait_ge(mm, mm_upB[b])
                                s_.copy(fup_ap(par, nuA + b), pb[4 + b][:, :])

                        @blk.sync
                        def _(sy, t=t, mm_upB=mm_upB, mm_upA=mm_upA):
                            if t + 1 < NFIT:
                                if mm_upB:
                                    sy.wait_ge(mm, mm_upB[-1])
                                elif mm_upA:
                                    sy.wait_ge(mm, mm_upA[-1])
                                gu_load_emit(t + 1, 1)(sy)

                        @blk.vector
                        def _(v):
                            pass
                if gmark is not None:
                    ld_marks.append((gmark, umark))

            for et in range(NDET):
                ld_marks = []
                for sub in range(8):
                    ld_marks.append(bump("s_ld", 32))
                sub_mm = [bump("mm") for _ in range(8)]
                with nc.Block() as blk:
                    @blk.sync
                    def _(sy, et=et, sub_mm=sub_mm):
                        if et == 0:
                            sy.wait_ge(ccW, 3)
                        for sub in range(8):
                            if sub >= 4:
                                sy.wait_ge(mm, sub_mm[sub - 4])
                            base = sub * 2 * GSH
                            quarter = sub % 4
                            for q in range(2):
                                srcp = ag_out_d.ap()[base + q * GSH: base + (q + 1) * GSH, :] \
                                    .rearrange("(i two) g -> i (two g)", two=2)[
                                        :, et * DET:(et + 1) * DET] \
                                    .rearrange("(c p) e -> p c e", p=128)
                                sy.dma_start(
                                    out=wslX[:, quarter * 4096 + q * 4 * DET:
                                             quarter * 4096 + (q + 1) * 4 * DET]
                                    .rearrange("p (c e) -> p c e", c=4),
                                    in_=srcp).then_inc(s_ld, 16)

                    @blk.tensor
                    def _(t, ld_marks=ld_marks, sub_mm=sub_mm, jlist=jlist):
                        for sub in range(8):
                            quarter = sub % 4
                            t.wait_ge(s_ld, ld_marks[sub])
                            for jj in range(SCH):
                                bank = pb[4 + jj]
                                for c in range(8):
                                    ic = sub * 8 + c
                                    last = t.matmul(
                                        bank[:, :],
                                        big[:, ic * HW + jj * 128: ic * HW + (jj + 1) * 128],
                                        wslX[:, quarter * 4096 + c * DET:
                                             quarter * 4096 + (c + 1) * DET],
                                        start=(ic == 0), stop=(ic == ICH - 1))
                            last.then_inc(mm, 1)

                    @blk.vector
                    def _(v, et=et, sub_mm=sub_mm):
                        for jj in range(SCH):
                            v.wait_ge(mm, sub_mm[7])
                            v.tensor_copy(
                                aout[:, jj * D + et * DET: jj * D + (et + 1) * DET],
                                pb[4 + jj][:, :])

            with nc.Block() as blk:
                @blk.vector
                def _(v, jlist=jlist):
                    for jj, j in enumerate(jlist):
                        doff = jj * D
                        v.tensor_tensor(dump[:, :], aout[:, doff:doff + D],
                                        aout[:, doff:doff + D], op=ALU.mult)
                        v.reduce_sum(sums[:, j:j + 1], aout[:, doff:doff + D], axis=AX.X)
                        v.reduce_sum(ssq[:, j:j + 1], dump[:, :], axis=AX.X)

                @blk.scalar
                def _(s):
                    pass
            with nc.Block() as blk:
                @blk.vector
                def _(v, jlist=jlist):
                    for j in jlist:
                        v.tensor_scalar_mul(mvec[:, j:j + 1], sums[:, j:j + 1], 1.0 / D)
                    for j in jlist:
                        v.tensor_scalar_mul(vvec[:, j:j + 1], ssq[:, j:j + 1], 1.0 / D)
                    pad(v)
                    for j in jlist:
                        v.tensor_tensor(t2[:, j:j + 1], mvec[:, j:j + 1], mvec[:, j:j + 1],
                                        op=ALU.mult)
                    pad(v)
                    for j in jlist:
                        v.tensor_tensor(vvec[:, j:j + 1], vvec[:, j:j + 1], t2[:, j:j + 1],
                                        op=ALU.subtract)

                @blk.scalar
                def _(s):
                    pass
            with nc.Block() as blk:
                @blk.scalar
                def _(s, jlist=jlist):
                    s.activation(t2[:, jlist[0]:jlist[0] + SCH], vvec[:, jlist[0]:jlist[0] + SCH],
                                 AF.Sqrt, bias=epsv[:, 0:1])

                @blk.vector
                def _(v):
                    pass
            with nc.Block() as blk:
                @blk.vector
                def _(v, jlist=jlist):
                    v.reciprocal(rstd[:, jlist[0]:jlist[0] + SCH],
                                 t2[:, jlist[0]:jlist[0] + SCH])
                    pad(v)
                    for jj, j in enumerate(jlist):
                        doff = jj * D
                        v.tensor_scalar(aout[:, doff:doff + D], aout[:, doff:doff + D],
                                        mvec[:, j:j + 1], rstd[:, j:j + 1],
                                        op0=ALU.subtract, op1=ALU.mult)

                @blk.scalar
                def _(s):
                    pass
            for jj, j in enumerate(jlist):
                doff = jj * D
                ld_here = bump("s_ld", 16)
                with nc.Block() as blk:
                    @blk.sync
                    def _(sy, j=j):
                        sy.dma_start(out=xch[:, :],
                                     in_=x1sp.ap()[j * 128:(j + 1) * 128, :]).then_inc(s_ld, 16)

                    @blk.vector
                    def _(v, doff=doff, ld_here=ld_here):
                        v.wait_ge(s_ld, ld_here)
                        v.tensor_tensor(tls[:, :], aout[:, doff:doff + D], xch[:, :],
                                        op=ALU.add)
                st_here = bump("s_st", 16)
                with nc.Block() as blk:
                    @blk.sync
                    def _(sy, j=j, st_here=st_here):
                        sy.dma_start(out=out[j * 128:(j + 1) * 128, :],
                                     in_=tls[:, :]).then_inc(s_st, 16)
                        sy.wait_ge(s_st, st_here)

                    @blk.vector
                    def _(v):
                        pass

        with nc.Block() as blk:
            @blk.sync
            def _(sy):
                sy.wait_ge(s_st, C["s_st"])

            @blk.vector
            def _(v):
                pass

    return nc


def _stage(inputs, T=1024):
    X = np.asarray(inputs["L0_attn_X"])
    W_in = np.asarray(inputs["L0_attn_W_in"])
    b_in = np.asarray(inputs["L0_attn_b_in"])
    W_out = np.asarray(inputs["L0_attn_W_out"])
    b_out = np.asarray(inputs["L0_attn_b_out"])
    gw = np.asarray(inputs["L0_ffn_gate_w"])
    uw = np.asarray(inputs["L0_ffn_up_w"])
    dw = np.asarray(inputs["L0_ffn_down_w"])

    eperm = np.concatenate([np.arange(D, 2 * D), np.arange(2 * D, 3 * D), np.arange(0, D)])
    winT = np.ascontiguousarray(W_in.T[:, eperm], dtype=np.float32)
    binx = np.ascontiguousarray(b_in[eperm], dtype=np.float32)[None, :]
    sig = np.concatenate([np.arange(0, ROT, 2), np.arange(1, ROT, 2), np.arange(ROT, D)])
    woutT = np.ascontiguousarray(W_out.T[sig, :], dtype=np.float32)
    boutx = np.ascontiguousarray(b_out, dtype=np.float32)[None, :]
    ident = np.eye(128, dtype=np.float32)

    GSH = II // NCORES
    in_maps = []
    for c in range(NCORES):
        cos_t, sin_t = _rope_tables(T, c)
        in_maps.append({
            "x_in": np.ascontiguousarray(X[0, c * T:(c + 1) * T, :], dtype=np.float32),
            "winT": winT, "binx": binx, "woutT": woutT, "boutx": boutx,
            "gsh": np.ascontiguousarray(gw[c * GSH:(c + 1) * GSH, :], dtype=np.float32),
            "ush": np.ascontiguousarray(uw[c * GSH:(c + 1) * GSH, :], dtype=np.float32),
            "dsh": np.ascontiguousarray(dw[:, c * GSH:(c + 1) * GSH], dtype=np.float32),
            "cosx": cos_t, "sinx": sin_t, "identx": ident,
        })
    return in_maps


def kernel(**inputs):
    T = SEQ // NCORES
    in_maps = _stage(inputs, T)
    nc = build_nc(T)
    res = run_bass_kernel_spmd(nc, in_maps, core_ids=list(range(NCORES)))
    outp = np.concatenate([np.asarray(r["out"]) for r in res.results], axis=0)
    return outp.reshape(1, SEQ, D).astype(np.float32)


# version 12
# speedup vs baseline: 1.0257x; 1.0257x over previous
import numpy as np
import ml_dtypes
import concourse.bass as bass
import concourse.mybir as mybir
from concourse.bass_utils import run_bass_kernel_spmd

F32 = mybir.dt.float32
BF16 = mybir.dt.bfloat16
FP8 = mybir.dt.float8e4
AF = mybir.ActivationFunctionType
ALU = mybir.AluOpType
AX = mybir.AxisListType

NCORES = 8
D = 2048
E3 = 3 * D
II = 8192
ROT = 1024
HROT = 512
SEQ = 8192
EPS_LN = 1e-5
C_CLAMP = 448e-30
ET = 512
FIT = 512
DET = 512

DEBUG = False


def _rope_tables(T, core):
    j2 = np.arange(0, ROT, 2, dtype=np.float32)
    inv = (np.float32(1.0) / (np.float32(10000.0) ** (j2 / np.float32(ROT)))).astype(np.float32)
    pos = np.arange(core * T, (core + 1) * T, dtype=np.float32)
    fr = pos[:, None] * inv[None, :]
    return (np.cos(fr).astype(ml_dtypes.bfloat16),
            np.sin(fr).astype(ml_dtypes.bfloat16))


def build_nc(T=1024):
    SC = T // 128
    NH = 2 if SC >= 2 else 1
    SCH = SC // NH
    KCH = D // 128
    GSH = II // NCORES
    AG_ROWS = 2 * D + 2 * GSH
    ICH = II // 128
    NFIT = II // FIT
    NDET = D // DET
    HW = 128 * SCH
    BIGW = max(2 * SC * D, ICH * HW)
    UW = max(KCH * T, KCH * GSH)

    nc = bass.Bass("TRN2", target_bir_lowering=False, detect_race_conditions=False)

    x_in = nc.dram_tensor("x_in", [T, D], F32, kind="ExternalInput").ap()
    winT = nc.dram_tensor("winT", [D, E3], F32, kind="ExternalInput").ap()
    binx = nc.dram_tensor("binx", [1, E3], F32, kind="ExternalInput").ap()
    woutT = nc.dram_tensor("woutT", [D, D], F32, kind="ExternalInput").ap()
    boutx = nc.dram_tensor("boutx", [1, D], F32, kind="ExternalInput").ap()
    gsh = nc.dram_tensor("gsh", [GSH, D], F32, kind="ExternalInput").ap()
    ush = nc.dram_tensor("ush", [GSH, D], F32, kind="ExternalInput").ap()
    dsh = nc.dram_tensor("dsh", [D, GSH], F32, kind="ExternalInput").ap()
    cosx = nc.dram_tensor("cosx", [T, HROT], BF16, kind="ExternalInput").ap()
    sinx = nc.dram_tensor("sinx", [T, HROT], BF16, kind="ExternalInput").ap()
    identx = nc.dram_tensor("identx", [128, 128], F32, kind="ExternalInput").ap()
    out = nc.dram_tensor("out", [T, D], F32, kind="ExternalOutput").ap()

    ag_in_g = nc.dram_tensor("ag_in_g", [D, GSH], BF16)
    ag_in_u = nc.dram_tensor("ag_in_u", [D, GSH], BF16)
    ag_in_d = nc.dram_tensor("ag_in_d", [2 * GSH, GSH], BF16)
    ag_out_g = nc.dram_tensor("ag_out_g", [NCORES * D, GSH], BF16)
    ag_out_u = nc.dram_tensor("ag_out_u", [NCORES * D, GSH], BF16)
    ag_out_d = nc.dram_tensor("ag_out_d", [NCORES * 2 * GSH, GSH], BF16)
    ar_in = nc.dram_tensor("ar_in", [1, D], F32)
    ar_out = nc.dram_tensor("ar_out", [1, D], F32)
    x1sp = nc.dram_tensor("x1sp", [T, D], F32)

    dbg = {}
    if DEBUG:
        for nm, sh in [("dbg_a", [1, D]), ("dbg_x1", [T, D]), ("dbg_u", [T, D]),
                       ("dbg_ai", [T, D]), ("dbg_ut", [128, UW]), ("dbg_u0", [128, D]),
                       ("dbg_st", [128, 48]), ("dbg_x0", [128, D]),
                       ("dbg_rn", [128, 32]), ("dbg_tkv", [128, D]), ("dbg_u2", [128, D]),
                       ("dbg_l2", [128, 8])]:
            dbg[nm] = nc.dram_tensor(nm, sh, F32, kind="ExternalOutput").ap()

    from contextlib import ExitStack
    with ExitStack() as stack:
        sb = lambda n_, sh, dt: stack.enter_context(nc.sbuf_tensor(n_, sh, dt))
        ps = lambda n_, sh, dt: stack.enter_context(nc.psum_tensor(n_, sh, dt))
        sem = lambda n_: stack.enter_context(nc.semaphore(n_))

        uT = sb("uT", [128, UW], BF16)
        big = sb("big", [128, BIGW], BF16)
        aout = sb("aout", [128, max(SC * D, 16384)], BF16)
        wslX = sb("wslX", [128, 16384], BF16)
        dump = sb("dump", [128, D], F32)
        xch = sb("xch", [128, D], F32)
        tls = sb("tls", [128, D], F32)
        tabc = sb("tabc", [128, HROT], BF16)
        tabs = sb("tabs", [128, HROT], BF16)
        wdq = sb("wdq", [128, 6144], BF16)
        fh = [wdq[:, i * FIT:(i + 1) * FIT] for i in range(4)]
        qt = sb("qt", [128, D], FP8)
        bm = sb("bm", [128, 64], F32)
        rs2 = sb("rs2", [128, 64], F32)
        s2v = sb("s2v", [128, 64], F32)
        sums = sb("sums", [128, max(SC, 2)], F32)
        ssq = sb("ssq", [128, max(SC, 2)], F32)
        mvec = sb("mvec", [128, max(SC, 2)], F32)
        vvec = sb("vvec", [128, max(SC, 2)], F32)
        rstd = sb("rstd", [128, max(SC, 2)], F32)
        t2 = sb("t2", [128, max(SC, 2)], F32)
        rnb = sb("rnb", [128, max(SC, 2)], BF16)
        abc = sb("abc", [128, D], BF16)
        ab16 = abc[0:1, :]
        bsl = sb("bsl", [1, 2 * ET], BF16)
        epsv = sb("epsv", [128, 1], F32)
        ones1 = sb("ones1", [1, 128], BF16)
        identb = sb("identb", [128, 128], BF16)
        asb = dump[0:1, :]

        padt = sb("padt", [128, 128], BF16)
        winA = xch
        winB = tls

        pb = [ps(f"pb{i}", [128, 512], F32) for i in range(8)]
        ptr_b = pb[7][:, :].bitcast(BF16)
        ptr_b2 = pb[6][:, :].bitcast(BF16)

        s_ld = sem("s_ld")
        g_ld = sem("g_ld")
        g_ld2 = sem("g_ld2")
        s_st = sem("s_st")
        mm = sem("mm")
        ccW = sem("ccW")
        ccA = sem("ccA")

        C = {"s_ld": 0, "g_ld": 0, "g_ld2": 0, "s_st": 0, "mm": 0}

        def bump(k, n=1):
            C[k] += n
            return C[k]

        def pad(v):
            v.tensor_copy(padt[:, 0:64], padt[:, 64:128])
            v.tensor_copy(padt[:, 0:64], padt[:, 64:128])

        def emit_qdq(v, src_ap, dst_ap, W):
            nb = W // 32
            v.reduce_max(bm[:, 0:nb], src_ap.rearrange("p (b k) -> p b k", k=32),
                         axis=AX.X, apply_absolute_value=True)
            pad(v)
            v.tensor_scalar_max(bm[:, 0:nb], bm[:, 0:nb], C_CLAMP)
            pad(v)
            v.reciprocal(rs2[:, 0:nb], bm[:, 0:nb])
            v.tensor_scalar_mul(s2v[:, 0:nb], bm[:, 0:nb], 1.0 / 224.0)
            v.tensor_scalar_mul(rs2[:, 0:nb], rs2[:, 0:nb], 224.0)
            pad(v)
            rse = rs2[:, 0:nb].unsqueeze(2).broadcast_to([128, nb, 32])
            s2e = s2v[:, 0:nb].unsqueeze(2).broadcast_to([128, nb, 32])
            v.tensor_tensor(qt[:, 0:W].rearrange("p (b k) -> p b k", k=32),
                            src_ap.rearrange("p (b k) -> p b k", k=32), rse, op=ALU.mult)
            v.tensor_tensor(dst_ap.rearrange("p (b k) -> p b k", k=32),
                            qt[:, 0:W].rearrange("p (b k) -> p b k", k=32), s2e,
                            op=ALU.mult)

        def tr_blocks(n128, src_fn, dst_fn):
            for b0 in range(0, (n128 + 7) // 8, 2):
                nb = min(2, (n128 + 7) // 8 - b0)
                mvs = [bump("mm") for _ in range(nb)]
                with nc.Block() as blk:
                    @blk.tensor
                    def _(t, b0=b0, nb=nb):
                        for bb in range(nb):
                            ptr = ptr_b if bb == 0 else ptr_b2
                            k0 = (b0 + bb) * 8
                            kn = min(8, n128 - k0)
                            for a in range(kn):
                                last = t.transpose(ptr[:, a * 128:(a + 1) * 128],
                                                   src_fn(k0 + a), identb[:, :])
                            last.then_inc(mm, 1)

                    @blk.vector
                    def _(v, b0=b0, nb=nb, mvs=mvs):
                        for bb in range(nb):
                            ptr = ptr_b if bb == 0 else ptr_b2
                            k0 = (b0 + bb) * 8
                            kn = min(8, n128 - k0)
                            v.wait_ge(mm, mvs[bb])
                            for a in range(kn):
                                v.tensor_copy(dst_fn(k0 + a), ptr[:, a * 128:(a + 1) * 128])

        def ln_stats(src_ap, j, out_u=None):
            with nc.Block() as blk:
                @blk.vector
                def _(v, j=j):
                    v.reduce_sum(sums[:, j:j + 1], src_ap, axis=AX.X)

                @blk.scalar
                def _(s, j=j):
                    s.activation(dump[:, :], src_ap, AF.Square)
            with nc.Block() as blk:
                @blk.vector
                def _(v, j=j):
                    v.reduce_sum(ssq[:, j:j + 1], dump[:, :], axis=AX.X)
                    v.tensor_scalar_mul(mvec[:, j:j + 1], sums[:, j:j + 1], 1.0 / D)
                    pad(v)
                    v.tensor_scalar_mul(vvec[:, j:j + 1], ssq[:, j:j + 1], 1.0 / D)
                    v.tensor_tensor(t2[:, j:j + 1], mvec[:, j:j + 1], mvec[:, j:j + 1],
                                    op=ALU.mult)
                    pad(v)
                    v.tensor_tensor(vvec[:, j:j + 1], vvec[:, j:j + 1], t2[:, j:j + 1],
                                    op=ALU.subtract)

                @blk.scalar
                def _(s):
                    pass
            with nc.Block() as blk:
                @blk.scalar
                def _(s, j=j):
                    s.activation(t2[:, j:j + 1], vvec[:, j:j + 1], AF.Sqrt,
                                 bias=epsv[:, 0:1])

                @blk.vector
                def _(v):
                    pass
            if out_u is not None:
                with nc.Block() as blk:
                    @blk.vector
                    def _(v, j=j):
                        v.reciprocal(rstd[:, j:j + 1], t2[:, j:j + 1])
                        pad(v)
                        v.tensor_scalar(out_u, src_ap, mvec[:, j:j + 1], rstd[:, j:j + 1],
                                        op0=ALU.subtract, op1=ALU.mult)

                    @blk.scalar
                    def _(s):
                        pass

        NWCH = 32

        def wchunk(i):
            if i < 8:
                return "g", i
            if i < 16:
                return "u", i - 8
            return "d", i - 16

        wload_marks = [None] * NWCH

        def w_load(sy, i):
            if i < 0 or i >= NWCH:
                return
            kind, p = wchunk(i)
            src = {"g": gsh, "u": ush, "d": dsh}[kind]
            W = GSH if kind == "d" else D
            buf = winA if i % 2 == 0 else winB
            wload_marks[i] = bump("s_ld", 16)
            sy.dma_start(out=buf[:, 0:W],
                         in_=src[p * 128:(p + 1) * 128, 0:W]).then_inc(s_ld, 16)

        def w_qdq(v, i):
            if i < 0 or i >= NWCH:
                return
            kind, p = wchunk(i)
            W = GSH if kind == "d" else D
            buf = winA if i % 2 == 0 else winB
            q = i % 3
            v.wait_ge(s_ld, wload_marks[i])
            emit_qdq(v, buf[:, 0:W], wdq[:, q * 2048: q * 2048 + W], W)

        def w_ntrb(i):
            if i < 0 or i >= NWCH:
                return 0
            return 1 if wchunk(i)[0] == "d" else 2

        def w_tr_emit(t, i):
            if i < 0 or i >= NWCH:
                return
            q = i % 3
            for b in range(w_ntrb(i)):
                ptr = ptr_b if b == 0 else ptr_b2
                for a in range(8):
                    k = b * 8 + a
                    last = t.transpose(
                        ptr[:, a * 128:(a + 1) * 128],
                        wdq[:, q * 2048 + k * 128: q * 2048 + (k + 1) * 128],
                        identb[:, :])
                last.then_inc(mm, 1)

        def w_tr_evac(v, i, trm):
            if i < 0 or i >= NWCH:
                return
            q = i % 3
            for b in range(len(trm)):
                ptr = ptr_b if b == 0 else ptr_b2
                v.wait_ge(mm, trm[b])
                for a in range(8):
                    k = b * 8 + a
                    v.tensor_copy(wdq[:, q * 2048 + k * 128: q * 2048 + (k + 1) * 128],
                                  ptr[:, a * 128:(a + 1) * 128])

        def w_stage(sy, i):
            if i < 0 or i >= NWCH:
                return
            kind, p = wchunk(i)
            q = i % 3
            if kind == "d":
                mark = bump("s_st", 32)
                for half in range(2):
                    dst = ag_in_d.ap() \
                        .rearrange("(i two) g -> i two g", two=2)[
                            half * 512:(half + 1) * 512, p // 8:p // 8 + 1,
                            (p % 8) * 128:(p % 8) * 128 + 128] \
                        .rearrange("(c ip) one r -> ip (c one) r", ip=128)
                    sy.dma_start(
                        out=dst,
                        in_=wdq[:, q * 2048 + half * 512: q * 2048 + (half + 1) * 512]
                        .rearrange("ip (c r) -> ip c r", r=128)).then_inc(s_st, 16)
            else:
                mark = bump("s_st", 64)
                ag = ag_in_g if kind == "g" else ag_in_u
                for qq in range(4):
                    dst = ag.ap()[qq * 512:(qq + 1) * 512, p * 128:(p + 1) * 128] \
                        .rearrange("(k r) c -> r k c", r=128)
                    sy.dma_start(
                        out=dst,
                        in_=wdq[:, q * 2048 + qq * 512: q * 2048 + (qq + 1) * 512]
                        .rearrange("r (k c) -> r k c", c=128)).then_inc(s_st, 16)
            sy.wait_ge(s_st, mark)

        def w_kick(ag_i, ag_o):
            def f(g):
                g.collective_compute(
                    "AllGather", ALU.bypass, replica_groups=[list(range(NCORES))],
                    ins=[ag_i.ap().opt()], outs=[ag_o.ap().opt()],
                ).then_inc(ccW, 1)
            return f

        WS = [0]

        with nc.Block() as blk:
            @blk.gpsimd
            def _(g):
                g.dma_start(out=identb[:, :], in_=identx).then_inc(g_ld, 16)
                bump("g_ld", 16)
                g.wait_ge(g_ld, C["g_ld"])
                g.memset(ones1[:, :], 1.0)
                g.memset(epsv[:, :], EPS_LN)
                g.memset(padt[:, :], 0.0)

            @blk.vector
            def _(v):
                pass

        UO = 0
        KQ = SC * D
        for j in range(SC):
            ld_here = bump("s_ld", 16)
            with nc.Block() as blk:
                @blk.sync
                def _(sy, j=j):
                    sy.dma_start(out=xch[:, :],
                                 in_=x_in[j * 128:(j + 1) * 128, :]).then_inc(s_ld, 16)

                @blk.vector
                def _(v, j=j, ld_here=ld_here):
                    v.wait_ge(s_ld, ld_here)
                    v.tensor_tensor(dump[:, :], xch[:, :], xch[:, :], op=ALU.mult)
                    v.reduce_sum(sums[:, j:j + 1], xch[:, :], axis=AX.X)
                    v.reduce_sum(ssq[:, j:j + 1], dump[:, :], axis=AX.X)
                    v.tensor_copy(big[:, UO + j * D: UO + (j + 1) * D], xch[:, :])
        with nc.Block() as blk:
            @blk.vector
            def _(v):
                for j in range(SC):
                    v.tensor_scalar_mul(mvec[:, j:j + 1], sums[:, j:j + 1], 1.0 / D)
                for j in range(SC):
                    v.tensor_scalar_mul(vvec[:, j:j + 1], ssq[:, j:j + 1], 1.0 / D)
                pad(v)
                for j in range(SC):
                    v.tensor_tensor(t2[:, j:j + 1], mvec[:, j:j + 1], mvec[:, j:j + 1],
                                    op=ALU.mult)
                pad(v)
                for j in range(SC):
                    v.tensor_tensor(vvec[:, j:j + 1], vvec[:, j:j + 1], t2[:, j:j + 1],
                                    op=ALU.subtract)

            @blk.scalar
            def _(s):
                pass
        with nc.Block() as blk:
            @blk.scalar
            def _(s):
                s.activation(t2[:, 0:SC], vvec[:, 0:SC], AF.Sqrt, bias=epsv[:, 0:1])

            @blk.vector
            def _(v):
                pass
        with nc.Block() as blk:
            @blk.vector
            def _(v):
                v.reciprocal(rstd[:, 0:SC], t2[:, 0:SC])
                pad(v)
                for j in range(SC):
                    v.tensor_scalar(big[:, UO + j * D: UO + (j + 1) * D],
                                    big[:, UO + j * D: UO + (j + 1) * D],
                                    mvec[:, j:j + 1], rstd[:, j:j + 1],
                                    op0=ALU.subtract, op1=ALU.mult)

            @blk.scalar
            def _(s):
                pass
        for j in range(SC):
            tr_blocks(KCH,
                      lambda k, j=j: big[:, UO + j * D + k * 128: UO + j * D + (k + 1) * 128],
                      lambda k, j=j: uT[:, k * T + j * 128: k * T + (j + 1) * 128])

        def qkv_load_plan(ti):
            et = ti if ti < 12 else ti - 12
            use_wout = ti >= 12
            buf = ti % 2
            g_here = bump("g_ld", 80)
            wsrc = woutT if use_wout else winT
            bsrc = boutx if use_wout else binx

            def emit(g, et=et, buf=buf, wsrc=wsrc, bsrc=bsrc):
                for q in range(4):
                    src3 = wsrc[q * 512:(q + 1) * 512, et * ET:(et + 1) * ET] \
                        .rearrange("(k r) e -> r k e", r=128)
                    dst3 = wslX[:, buf * 8192 + q * 4 * ET: buf * 8192 + (q + 1) * 4 * ET] \
                        .rearrange("r (k e) -> r k e", k=4)
                    g.dma_start(out=dst3, in_=src3).then_inc(g_ld, 16)
                g.dma_start(out=bsl[:, buf * ET:(buf + 1) * ET],
                            in_=bsrc[0:1, et * ET:(et + 1) * ET]).then_inc(g_ld, 16)
            return g_here, emit

        qkv_loads = [qkv_load_plan(ti) for ti in range(16)]

        def qkv_etile(ti, dst_desc, lhs_tensor=None, emit_next=True, gext=None):
            lhs_tensor = uT if lhs_tensor is None else lhs_tensor
            buf = ti % 2
            g_need = qkv_loads[ti][0]
            load_emit = qkv_loads[ti + 1][1] if (emit_next and ti + 1 < 16) else None
            for half, jlist_ in ((0, list(range(min(SC, 6)))), (1, list(range(6, SC)))):
                ci = WS[0]
                WS[0] += 1
                mmv = [bump("mm") for _ in jlist_]
                trm = [bump("mm") for _ in range(w_ntrb(ci - 1))]
                with nc.Block() as blk:
                    @blk.gpsimd
                    def _(g, half=half):
                        if half == 0 and load_emit is not None:
                            load_emit(g)
                        if half == 0 and gext is not None:
                            gext(g)

                    @blk.sync
                    def _(sy, ci=ci):
                        w_stage(sy, ci - 2)
                        w_load(sy, ci + 1)

                    @blk.tensor
                    def _(t, buf=buf, g_need=g_need, half=half, jlist_=jlist_, ci=ci):
                        if half == 0:
                            t.wait_ge(g_ld, g_need)
                        for j in jlist_:
                            bank = pb[j % 6]
                            for k in range(KCH):
                                t.matmul(bank[:, 0:ET],
                                         lhs_tensor[:, k * T + j * 128: k * T + (j + 1) * 128],
                                         wslX[:, buf * 8192 + k * ET: buf * 8192 + (k + 1) * ET],
                                         start=(k == 0), stop=False)
                            last = t.matmul(bank[:, 0:ET], ones1[:, :],
                                            bsl[0:1, buf * ET:(buf + 1) * ET],
                                            start=False, stop=True)
                            last.then_inc(mm, 1)
                        w_tr_emit(t, ci - 1)

                    @blk.vector
                    def _(v, jlist_=jlist_, mmv=mmv, trm=trm, ci=ci):
                        for jj, j in enumerate(jlist_):
                            v.wait_ge(mm, mmv[jj])
                            dst_desc(v, j, pb[j % 6])
                        w_qdq(v, ci)
                        w_tr_evac(v, ci - 1, trm)

        def evac_deint(dst, dbase, toff):
            def f(v, j, bank):
                src2 = bank[:, 0:ET].rearrange("p (a two) -> p two a", two=2)
                v.tensor_copy(dst[:, dbase + j * D + toff * 256:
                                  dbase + j * D + toff * 256 + 256], src2[:, 0, :])
                v.tensor_copy(dst[:, dbase + j * D + HROT + toff * 256:
                                  dbase + j * D + HROT + toff * 256 + 256], src2[:, 1, :])
            return f

        def evac_plain(dst, dbase, coloff):
            def f(v, j, bank):
                v.tensor_copy(dst[:, dbase + j * D + coloff: dbase + j * D + coloff + ET],
                              bank[:, 0:ET])
            return f

        def rope_pass(region_base, scr):
            for j in range(SC):
                ld_here = bump("s_ld", 32)
                with nc.Block() as blk:
                    @blk.sync
                    def _(sy, j=j):
                        sy.dma_start(out=tabc[:, :],
                                     in_=cosx[j * 128:(j + 1) * 128, :]).then_inc(s_ld, 16)
                        sy.dma_start(out=tabs[:, :],
                                     in_=sinx[j * 128:(j + 1) * 128, :]).then_inc(s_ld, 16)

                    @blk.vector
                    def _(v, j=j, ld_here=ld_here):
                        v.wait_ge(s_ld, ld_here)
                        e_ap = big[:, region_base + j * D: region_base + j * D + HROT]
                        o_ap = big[:, region_base + j * D + HROT:
                                   region_base + j * D + 2 * HROT]
                        rA = wdq[:, scr: scr + HROT]
                        rB = wdq[:, scr + HROT: scr + 2 * HROT]
                        rC = wdq[:, scr + 2 * HROT: scr + 3 * HROT]
                        rD = wdq[:, scr + 3 * HROT: scr + 4 * HROT]
                        v.tensor_tensor(rA, e_ap, tabc[:, :], op=ALU.mult)
                        v.tensor_tensor(rB, o_ap, tabs[:, :], op=ALU.mult)
                        v.tensor_tensor(rC, o_ap, tabc[:, :], op=ALU.mult)
                        v.tensor_tensor(rD, e_ap, tabs[:, :], op=ALU.mult)
                        v.tensor_tensor(e_ap, rA, rB, op=ALU.subtract)
                        v.tensor_tensor(o_ap, rC, rD, op=ALU.add)

        with nc.Block() as blk:
            @blk.gpsimd
            def _(g):
                qkv_loads[0][1](g)

            @blk.sync
            def _(sy):
                w_load(sy, 0)

            @blk.vector
            def _(v):
                pass

        for ti in range(0, 2):
            qkv_etile(ti, evac_deint(big, KQ, ti))
        for ti in range(2, 4):
            qkv_etile(ti, evac_plain(big, KQ, ROT + (ti - 2) * ET))
        rope_pass(KQ, 2 * 2048)

        for j in range(SC):
            with nc.Block() as blk:
                @blk.scalar
                def _(s, j=j):
                    s.activation(dump[:, :], big[:, KQ + j * D: KQ + (j + 1) * D],
                                 AF.Square)

                @blk.vector
                def _(v):
                    pass
            with nc.Block() as blk:
                @blk.vector
                def _(v, j=j):
                    v.reduce_sum(ssq[:, j:j + 1], dump[:, :], axis=AX.X)

                @blk.scalar
                def _(s):
                    pass
        with nc.Block() as blk:
            @blk.scalar
            def _(s):
                s.activation(sums[:, 0:SC], ssq[:, 0:SC], AF.Sqrt)

            @blk.vector
            def _(v):
                pass
        with nc.Block() as blk:
            @blk.vector
            def _(v):
                v.tensor_scalar_max(sums[:, 0:SC], sums[:, 0:SC], 1e-12)
                pad(v)
                v.reciprocal(t2[:, 0:SC], sums[:, 0:SC])
                pad(v)
                v.tensor_copy(rnb[:, 0:SC], t2[:, 0:SC])

            @blk.scalar
            def _(s):
                pass
        if DEBUG:
            with nc.Block() as blk:
                @blk.vector
                def _(v):
                    v.tensor_copy(dump[:, 0:SC], rnb[:, 0:SC])

                @blk.scalar
                def _(s):
                    pass
            gd = bump("g_ld2", 64)
            with nc.Block() as blk:
                @blk.gpsimd
                def _(g, gd=gd):
                    g.dma_start(out=dbg["dbg_rn"][:, 0:8], in_=ssq[:, 0:SC]).then_inc(g_ld2, 16)
                    g.dma_start(out=dbg["dbg_rn"][:, 8:16], in_=sums[:, 0:SC]).then_inc(g_ld2, 16)
                    g.dma_start(out=dbg["dbg_rn"][:, 16:24], in_=t2[:, 0:SC]).then_inc(g_ld2, 16)
                    g.dma_start(out=dbg["dbg_rn"][:, 24:32], in_=dump[:, 0:SC]).then_inc(g_ld2, 16)
                    g.wait_ge(g_ld2, gd)

                @blk.vector
                def _(v):
                    pass

        qkv_etile(4, evac_deint(aout, 0, 0))
        qkv_etile(5, evac_deint(aout, 0, 1), gext=w_kick(ag_in_g, ag_out_g))
        for ti in range(6, 8):
            qkv_etile(ti, evac_plain(aout, 0, ROT + (ti - 6) * ET))

        for j in range(SC):
            with nc.Block() as blk:
                @blk.vector
                def _(v, j=j):
                    v.tensor_tensor(aout[:, j * D:(j + 1) * D],
                                    big[:, KQ + j * D: KQ + (j + 1) * D],
                                    aout[:, j * D:(j + 1) * D], op=ALU.mult)

                @blk.tensor
                def _(t):
                    pass
        if DEBUG:
            gd = bump("g_ld2", 16)
            with nc.Block() as blk:
                @blk.gpsimd
                def _(g, gd=gd):
                    g.dma_start(out=dbg["dbg_tkv"], in_=aout[:, 0:D]).then_inc(g_ld2, 16)
                    g.wait_ge(g_ld2, gd)

                @blk.vector
                def _(v):
                    pass
        for vs in range(4):
            mv = bump("mm")
            with nc.Block() as blk:
                @blk.tensor
                def _(t, vs=vs):
                    for j in range(SC):
                        last = t.matmul(pb[6][0:1, :], rnb[:, j:j + 1],
                                        aout[:, j * D + vs * 512: j * D + (vs + 1) * 512],
                                        start=(j == 0), stop=(j == SC - 1))
                    last.then_inc(mm, 1)

                @blk.vector
                def _(v, vs=vs, mv=mv):
                    v.wait_ge(mm, mv)
                    v.tensor_copy(asb[0:1, vs * 512:(vs + 1) * 512], pb[6][0:1, :])

        g1 = bump("g_ld2", 16)
        with nc.Block() as blk:
            @blk.gpsimd
            def _(g, g1=g1):
                g.dma_start(out=ar_in.ap(), in_=asb[:, :]).then_inc(g_ld2, 16)
                g.wait_ge(g_ld2, g1)
                g.collective_compute(
                    "AllReduce", ALU.add, replica_groups=[list(range(NCORES))],
                    ins=[ar_in.ap().opt()], outs=[ar_out.ap().opt()],
                ).then_inc(ccA, 1)

            @blk.vector
            def _(v):
                pass

        qkv_etile(8, evac_deint(big, KQ, 0))
        qkv_etile(9, evac_deint(big, KQ, 1), gext=w_kick(ag_in_u, ag_out_u))
        for ti in range(10, 12):
            qkv_etile(ti, evac_plain(big, KQ, ROT + (ti - 10) * ET))
        rope_pass(KQ, 0)

        g2 = bump("g_ld2", 16)
        with nc.Block() as blk:
            @blk.gpsimd
            def _(g):
                g.wait_ge(ccA, 1)
                g.dma_start(out=asb[:, :], in_=ar_out.ap()).then_inc(g_ld2, 16)

            @blk.vector
            def _(v, g2=g2):
                v.wait_ge(g_ld2, g2)
                v.tensor_copy(ab16[:, :], asb[:, :])
        mm4 = [bump("mm") for _ in range(4)]
        with nc.Block() as blk:
            @blk.tensor
            def _(t):
                for q in range(4):
                    t.matmul(pb[q][:, :], ones1[:, :], ab16[0:1, q * 512:(q + 1) * 512],
                             start=True, stop=True).then_inc(mm, 1)

            @blk.vector
            def _(v, mm4=mm4):
                for q in range(4):
                    v.wait_ge(mm, mm4[q])
                    v.tensor_copy(abc[:, q * 512:(q + 1) * 512], pb[q][:, :])

        if DEBUG:
            st_here = bump("s_st", 16)
            with nc.Block() as blk:
                @blk.sync
                def _(sy, st_here=st_here):
                    sy.dma_start(out=dbg["dbg_a"], in_=asb[:, :]).then_inc(s_st, 16)
                    sy.wait_ge(s_st, st_here)

                @blk.vector
                def _(v):
                    pass

        for j in range(SC):
            with nc.Block() as blk:
                @blk.vector
                def _(v, j=j):
                    v.tensor_tensor(big[:, KQ + j * D: KQ + (j + 1) * D],
                                    big[:, KQ + j * D: KQ + (j + 1) * D],
                                    abc[:, :], op=ALU.mult)

                @blk.tensor
                def _(t):
                    pass
            tr_blocks(KCH,
                      lambda k, j=j: big[:, KQ + j * D + k * 128: KQ + j * D + (k + 1) * 128],
                      lambda k, j=j: uT[:, k * T + j * 128: k * T + (j + 1) * 128])

        def evac_outproj(et):
            def f(v, j, bank):
                v.tensor_tensor(aout[:, j * D + et * ET: j * D + (et + 1) * ET],
                                bank[:, 0:ET],
                                big[:, UO + j * D + et * ET: UO + j * D + (et + 1) * ET],
                                op=ALU.add)
            return f

        for ti in range(12, 16):
            qkv_etile(ti, evac_outproj(ti - 12), lhs_tensor=uT)

        trm_d = [bump("mm") for _ in range(w_ntrb(NWCH - 1))]
        with nc.Block() as blk:
            @blk.tensor
            def _(t):
                w_tr_emit(t, NWCH - 1)

            @blk.sync
            def _(sy):
                w_stage(sy, NWCH - 2)

            @blk.vector
            def _(v, trm_d=trm_d):
                w_tr_evac(v, NWCH - 1, trm_d)
        with nc.Block() as blk:
            @blk.sync
            def _(sy):
                w_stage(sy, NWCH - 1)

            @blk.vector
            def _(v):
                pass
        with nc.Block() as blk:
            @blk.gpsimd
            def _(g):
                w_kick(ag_in_d, ag_out_d)(g)

            @blk.vector
            def _(v):
                pass

        if DEBUG:
            gd = bump("g_ld2", 16 * SC)
            with nc.Block() as blk:
                @blk.gpsimd
                def _(g, gd=gd):
                    for j in range(SC):
                        g.dma_start(out=dbg["dbg_ai"][j * 128:(j + 1) * 128, :],
                                    in_=aout[:, j * D:(j + 1) * D]).then_inc(g_ld2, 16)
                    g.wait_ge(g_ld2, gd)

                @blk.vector
                def _(v):
                    pass

        with nc.Block() as blk:
            @blk.vector
            def _(v):
                for j in range(SC):
                    v.tensor_tensor(dump[:, :], aout[:, j * D:(j + 1) * D],
                                    aout[:, j * D:(j + 1) * D], op=ALU.mult)
                    v.reduce_sum(sums[:, j:j + 1], aout[:, j * D:(j + 1) * D], axis=AX.X)
                    v.reduce_sum(ssq[:, j:j + 1], dump[:, :], axis=AX.X)

            @blk.scalar
            def _(s):
                pass

        def ln_math_block():
            with nc.Block() as blk:
                @blk.vector
                def _(v):
                    for j in range(SC):
                        v.tensor_scalar_mul(mvec[:, j:j + 1], sums[:, j:j + 1], 1.0 / D)
                    for j in range(SC):
                        v.tensor_scalar_mul(vvec[:, j:j + 1], ssq[:, j:j + 1], 1.0 / D)
                    pad(v)
                    for j in range(SC):
                        v.tensor_tensor(t2[:, j:j + 1], mvec[:, j:j + 1], mvec[:, j:j + 1],
                                        op=ALU.mult)
                    pad(v)
                    for j in range(SC):
                        v.tensor_tensor(vvec[:, j:j + 1], vvec[:, j:j + 1], t2[:, j:j + 1],
                                        op=ALU.subtract)

                @blk.scalar
                def _(s):
                    pass
            with nc.Block() as blk:
                @blk.scalar
                def _(s):
                    s.activation(t2[:, 0:SC], vvec[:, 0:SC], AF.Sqrt, bias=epsv[:, 0:1])

                @blk.vector
                def _(v):
                    pass
            with nc.Block() as blk:
                @blk.vector
                def _(v):
                    v.reciprocal(rstd[:, 0:SC], t2[:, 0:SC])
                    pad(v)

                @blk.scalar
                def _(s):
                    pass

        ln_math_block()
        with nc.Block() as blk:
            @blk.vector
            def _(v):
                for j in range(SC):
                    v.tensor_scalar(aout[:, j * D:(j + 1) * D], aout[:, j * D:(j + 1) * D],
                                    mvec[:, j:j + 1], rstd[:, j:j + 1],
                                    op0=ALU.subtract, op1=ALU.mult)

            @blk.scalar
            def _(s):
                pass
        for j in range(SC):
            ld_here = bump("s_ld", 16)
            with nc.Block() as blk:
                @blk.sync
                def _(sy, j=j):
                    sy.dma_start(out=xch[:, :],
                                 in_=x_in[j * 128:(j + 1) * 128, :]).then_inc(s_ld, 16)

                @blk.vector
                def _(v, j=j, ld_here=ld_here):
                    v.wait_ge(s_ld, ld_here)
                    v.tensor_tensor(tls[:, :], aout[:, j * D:(j + 1) * D], xch[:, :],
                                    op=ALU.add)
                    v.tensor_tensor(dump[:, :], tls[:, :], tls[:, :], op=ALU.mult)
                    v.reduce_sum(sums[:, j:j + 1], tls[:, :], axis=AX.X)
                    v.reduce_sum(ssq[:, j:j + 1], dump[:, :], axis=AX.X)
            st_here = bump("s_st", 16)
            with nc.Block() as blk:
                @blk.sync
                def _(sy, j=j, st_here=st_here):
                    sy.dma_start(out=x1sp.ap()[j * 128:(j + 1) * 128, :],
                                 in_=tls[:, :]).then_inc(s_st, 16)
                    sy.wait_ge(s_st, st_here)

                @blk.vector
                def _(v):
                    pass
        ln_math_block()
        for j in range(SC):
            ld_here = bump("s_ld", 16)
            with nc.Block() as blk:
                @blk.sync
                def _(sy, j=j):
                    sy.dma_start(out=xch[:, :],
                                 in_=x1sp.ap()[j * 128:(j + 1) * 128, :]).then_inc(s_ld, 16)

                @blk.vector
                def _(v, j=j, ld_here=ld_here):
                    v.wait_ge(s_ld, ld_here)
                    v.tensor_scalar(wslX[:, 0:D], xch[:, :], mvec[:, j:j + 1],
                                    rstd[:, j:j + 1], op0=ALU.subtract, op1=ALU.mult)
                    emit_qdq(v, wslX[:, 0:D], wslX[:, 0:D], D)
            tr_blocks(KCH,
                      lambda k: wslX[:, k * 128:(k + 1) * 128],
                      lambda k, j=j: uT[:, k * T + j * 128: k * T + (j + 1) * 128])

        for h in range(NH):
            jlist = list(range(h * SCH, (h + 1) * SCH))

            def gu_load_emit(it, gi):
                def emit(sy, it=it, gi=gi):
                    i0 = it * FIT
                    r = i0 // GSH
                    iL = i0 - r * GSH
                    agsrc = ag_out_g if gi == 0 else ag_out_u
                    for q in range(4):
                        src = agsrc.ap()[r * D + q * 512:
                                         r * D + (q + 1) * 512, iL:iL + FIT] \
                            .rearrange("(k p) i -> p k i", p=128)
                        sy.dma_start(out=wslX[:, gi * 8192 + q * 2048:
                                              gi * 8192 + (q + 1) * 2048]
                                     .rearrange("p (k i) -> p k i", k=4),
                                     in_=src).then_inc(s_ld, 16)
                return emit

            g0 = bump("s_ld", 64)
            u0 = bump("s_ld", 64)
            with nc.Block() as blk:
                @blk.sync
                def _(sy, h=h):
                    if h == 0:
                        sy.wait_ge(ccW, 2)
                    gu_load_emit(0, 0)(sy)
                    gu_load_emit(0, 1)(sy)

                @blk.vector
                def _(v):
                    pass
            FS = SC * D // 2 if SC * D // 2 >= 8192 else 8192
            FS = 8192

            def fsil_ap(par, jj):
                return aout[:, FS + par * 4096 + jj * FIT: FS + par * 4096 + (jj + 1) * FIT]

            def fup_ap(par, jj):
                return aout[:, FS + par * 4096 + 2048 + jj * FIT:
                            FS + par * 4096 + 2048 + (jj + 1) * FIT]

            def fhp(it, jj):
                return wdq[:, (it % 2) * 2048 + jj * FIT:
                           (it % 2) * 2048 + (jj + 1) * FIT]

            nuA = min(SCH, 2)
            nuB = SCH - nuA
            ld_marks = [(g0, u0)]
            for t in range(NFIT + 2):
                fills = t < NFIT
                it1 = t - 1
                it2 = t - 2
                ntrb = (SCH * (FIT // 128) + 7) // 8 if 0 <= it2 < NFIT else 0
                mm_gate = [bump("mm") for _ in range(SCH)] if fills else []
                mm_upA = [bump("mm") for _ in range(nuA)] if fills else []
                mm_tr = [bump("mm") for _ in range(ntrb)]
                gmark = bump("s_ld", 64) if t + 1 < NFIT else None
                with nc.Block() as blk:
                    @blk.tensor
                    def _(t_, t=t, it2=it2, fills=fills, jlist=jlist):
                        if fills:
                            t_.wait_ge(s_ld, ld_marks[t][0])
                            for jj, j in enumerate(jlist):
                                for k in range(KCH):
                                    last = t_.matmul(
                                        pb[jj][:, :],
                                        uT[:, k * T + j * 128: k * T + (j + 1) * 128],
                                        wslX[:, k * FIT:(k + 1) * FIT],
                                        start=(k == 0), stop=(k == KCH - 1))
                                last.then_inc(mm, 1)
                            t_.wait_ge(s_ld, ld_marks[t][1])
                            for jj in range(nuA):
                                j = jlist[jj]
                                for k in range(KCH):
                                    last = t_.matmul(
                                        pb[4 + jj][:, :],
                                        uT[:, k * T + j * 128: k * T + (j + 1) * 128],
                                        wslX[:, 8192 + k * FIT: 8192 + (k + 1) * FIT],
                                        start=(k == 0), stop=(k == KCH - 1))
                                last.then_inc(mm, 1)
                        if 0 <= it2 < NFIT:
                            for b in range((SCH * (FIT // 128) + 7) // 8):
                                ptr = ptr_b if b == 0 else ptr_b2
                                for a in range(min(8, SCH * (FIT // 128) - b * 8)):
                                    jj, kk = divmod(b * 8 + a, FIT // 128)
                                    last = t_.transpose(
                                        ptr[:, a * 128:(a + 1) * 128],
                                        fhp(it2, jj)[:, kk * 128:(kk + 1) * 128],
                                        identb[:, :])
                                last.then_inc(mm, 1)

                    @blk.scalar
                    def _(s_, t=t, fills=fills, mm_gate=mm_gate, mm_upA=mm_upA):
                        if not fills:
                            return
                        par = t % 2
                        for jj in range(SCH):
                            s_.wait_ge(mm, mm_gate[jj])
                            s_.activation(fsil_ap(par, jj), pb[jj][:, :], AF.Silu)
                        for jj in range(nuA):
                            s_.wait_ge(mm, mm_upA[jj])
                            s_.copy(fup_ap(par, jj), pb[4 + jj][:, :])

                    @blk.sync
                    def _(sy, t=t, mm_gate=mm_gate):
                        if t + 1 < NFIT:
                            sy.wait_ge(mm, mm_gate[SCH - 1])
                            gu_load_emit(t + 1, 0)(sy)

                    @blk.vector
                    def _(v, t=t, it1=it1, it2=it2, mm_tr=mm_tr):
                        if 0 <= it1 < NFIT:
                            par = it1 % 2
                            for jj in range(SCH):
                                v.tensor_tensor(fhp(it1, jj), fsil_ap(par, jj),
                                                fup_ap(par, jj), op=ALU.mult)
                                emit_qdq(v, fhp(it1, jj), fhp(it1, jj), FIT)
                        if 0 <= it2 < NFIT:
                            for b in range(len(mm_tr)):
                                ptr = ptr_b if b == 0 else ptr_b2
                                v.wait_ge(mm, mm_tr[b])
                                for a in range(min(8, SCH * (FIT // 128) - b * 8)):
                                    jj, kk = divmod(b * 8 + a, FIT // 128)
                                    v.tensor_copy(
                                        big[:, (it2 * (FIT // 128) + kk) * HW
                                            + jj * 128:
                                            (it2 * (FIT // 128) + kk) * HW
                                            + (jj + 1) * 128],
                                        ptr[:, a * 128:(a + 1) * 128])
                mm_upB = [bump("mm") for _ in range(nuB)] if fills else []
                umark = bump("s_ld", 64) if t + 1 < NFIT else None
                if (fills and nuB) or t + 1 < NFIT:
                    with nc.Block() as blk:
                        @blk.tensor
                        def _(t_, t=t, fills=fills, jlist=jlist):
                            if not fills:
                                return
                            for b in range(nuB):
                                jj = nuA + b
                                j = jlist[jj]
                                for k in range(KCH):
                                    last = t_.matmul(
                                        pb[4 + b][:, :],
                                        uT[:, k * T + j * 128: k * T + (j + 1) * 128],
                                        wslX[:, 8192 + k * FIT: 8192 + (k + 1) * FIT],
                                        start=(k == 0), stop=(k == KCH - 1))
                                last.then_inc(mm, 1)

                        @blk.scalar
                        def _(s_, t=t, fills=fills, mm_upB=mm_upB):
                            if not fills:
                                return
                            par = t % 2
                            for b in range(nuB):
                                s_.wait_ge(mm, mm_upB[b])
                                s_.copy(fup_ap(par, nuA + b), pb[4 + b][:, :])

                        @blk.sync
                        def _(sy, t=t, mm_upB=mm_upB, mm_upA=mm_upA):
                            if t + 1 < NFIT:
                                if mm_upB:
                                    sy.wait_ge(mm, mm_upB[-1])
                                elif mm_upA:
                                    sy.wait_ge(mm, mm_upA[-1])
                                gu_load_emit(t + 1, 1)(sy)

                        @blk.vector
                        def _(v):
                            pass
                if gmark is not None:
                    ld_marks.append((gmark, umark))

            for et in range(NDET):
                ld_marks = []
                for sub in range(8):
                    ld_marks.append(bump("s_ld", 32))
                sub_mm = [bump("mm") for _ in range(8)]
                with nc.Block() as blk:
                    @blk.sync
                    def _(sy, et=et, sub_mm=sub_mm):
                        if et == 0:
                            sy.wait_ge(ccW, 3)
                        for sub in range(8):
                            if sub >= 4:
                                sy.wait_ge(mm, sub_mm[sub - 4])
                            base = sub * 2 * GSH
                            quarter = sub % 4
                            for q in range(2):
                                srcp = ag_out_d.ap()[base + q * GSH: base + (q + 1) * GSH, :] \
                                    .rearrange("(i two) g -> i (two g)", two=2)[
                                        :, et * DET:(et + 1) * DET] \
                                    .rearrange("(c p) e -> p c e", p=128)
                                sy.dma_start(
                                    out=wslX[:, quarter * 4096 + q * 4 * DET:
                                             quarter * 4096 + (q + 1) * 4 * DET]
                                    .rearrange("p (c e) -> p c e", c=4),
                                    in_=srcp).then_inc(s_ld, 16)

                    @blk.tensor
                    def _(t, ld_marks=ld_marks, sub_mm=sub_mm, jlist=jlist):
                        for sub in range(8):
                            quarter = sub % 4
                            t.wait_ge(s_ld, ld_marks[sub])
                            for jj in range(SCH):
                                bank = pb[4 + jj]
                                for c in range(8):
                                    ic = sub * 8 + c
                                    last = t.matmul(
                                        bank[:, :],
                                        big[:, ic * HW + jj * 128: ic * HW + (jj + 1) * 128],
                                        wslX[:, quarter * 4096 + c * DET:
                                             quarter * 4096 + (c + 1) * DET],
                                        start=(ic == 0), stop=(ic == ICH - 1))
                            last.then_inc(mm, 1)

                    @blk.vector
                    def _(v, et=et, sub_mm=sub_mm):
                        for jj in range(SCH):
                            v.wait_ge(mm, sub_mm[7])
                            v.tensor_copy(
                                aout[:, jj * D + et * DET: jj * D + (et + 1) * DET],
                                pb[4 + jj][:, :])

            with nc.Block() as blk:
                @blk.vector
                def _(v, jlist=jlist):
                    for jj, j in enumerate(jlist):
                        doff = jj * D
                        v.tensor_tensor(dump[:, :], aout[:, doff:doff + D],
                                        aout[:, doff:doff + D], op=ALU.mult)
                        v.reduce_sum(sums[:, j:j + 1], aout[:, doff:doff + D], axis=AX.X)
                        v.reduce_sum(ssq[:, j:j + 1], dump[:, :], axis=AX.X)

                @blk.scalar
                def _(s):
                    pass
            with nc.Block() as blk:
                @blk.vector
                def _(v, jlist=jlist):
                    for j in jlist:
                        v.tensor_scalar_mul(mvec[:, j:j + 1], sums[:, j:j + 1], 1.0 / D)
                    for j in jlist:
                        v.tensor_scalar_mul(vvec[:, j:j + 1], ssq[:, j:j + 1], 1.0 / D)
                    pad(v)
                    for j in jlist:
                        v.tensor_tensor(t2[:, j:j + 1], mvec[:, j:j + 1], mvec[:, j:j + 1],
                                        op=ALU.mult)
                    pad(v)
                    for j in jlist:
                        v.tensor_tensor(vvec[:, j:j + 1], vvec[:, j:j + 1], t2[:, j:j + 1],
                                        op=ALU.subtract)

                @blk.scalar
                def _(s):
                    pass
            with nc.Block() as blk:
                @blk.scalar
                def _(s, jlist=jlist):
                    s.activation(t2[:, jlist[0]:jlist[0] + SCH], vvec[:, jlist[0]:jlist[0] + SCH],
                                 AF.Sqrt, bias=epsv[:, 0:1])

                @blk.vector
                def _(v):
                    pass
            with nc.Block() as blk:
                @blk.vector
                def _(v, jlist=jlist):
                    v.reciprocal(rstd[:, jlist[0]:jlist[0] + SCH],
                                 t2[:, jlist[0]:jlist[0] + SCH])
                    pad(v)
                    for jj, j in enumerate(jlist):
                        doff = jj * D
                        v.tensor_scalar(aout[:, doff:doff + D], aout[:, doff:doff + D],
                                        mvec[:, j:j + 1], rstd[:, j:j + 1],
                                        op0=ALU.subtract, op1=ALU.mult)

                @blk.scalar
                def _(s):
                    pass
            for jj, j in enumerate(jlist):
                doff = jj * D
                ld_here = bump("s_ld", 16)
                with nc.Block() as blk:
                    @blk.sync
                    def _(sy, j=j):
                        sy.dma_start(out=xch[:, :],
                                     in_=x1sp.ap()[j * 128:(j + 1) * 128, :]).then_inc(s_ld, 16)

                    @blk.vector
                    def _(v, doff=doff, ld_here=ld_here):
                        v.wait_ge(s_ld, ld_here)
                        v.tensor_tensor(tls[:, :], aout[:, doff:doff + D], xch[:, :],
                                        op=ALU.add)
                st_here = bump("s_st", 16)
                with nc.Block() as blk:
                    @blk.sync
                    def _(sy, j=j, st_here=st_here):
                        sy.dma_start(out=out[j * 128:(j + 1) * 128, :],
                                     in_=tls[:, :]).then_inc(s_st, 16)
                        sy.wait_ge(s_st, st_here)

                    @blk.vector
                    def _(v):
                        pass

        with nc.Block() as blk:
            @blk.sync
            def _(sy):
                sy.wait_ge(s_st, C["s_st"])

            @blk.vector
            def _(v):
                pass

    return nc


def _stage(inputs, T=1024):
    X = np.asarray(inputs["L0_attn_X"])
    W_in = np.asarray(inputs["L0_attn_W_in"])
    b_in = np.asarray(inputs["L0_attn_b_in"])
    W_out = np.asarray(inputs["L0_attn_W_out"])
    b_out = np.asarray(inputs["L0_attn_b_out"])
    gw = np.asarray(inputs["L0_ffn_gate_w"])
    uw = np.asarray(inputs["L0_ffn_up_w"])
    dw = np.asarray(inputs["L0_ffn_down_w"])

    eperm = np.concatenate([np.arange(D, 2 * D), np.arange(2 * D, 3 * D), np.arange(0, D)])
    winT = np.ascontiguousarray(W_in.T[:, eperm], dtype=np.float32)
    binx = np.ascontiguousarray(b_in[eperm], dtype=np.float32)[None, :]
    sig = np.concatenate([np.arange(0, ROT, 2), np.arange(1, ROT, 2), np.arange(ROT, D)])
    woutT = np.ascontiguousarray(W_out.T[sig, :], dtype=np.float32)
    boutx = np.ascontiguousarray(b_out, dtype=np.float32)[None, :]
    ident = np.eye(128, dtype=np.float32)

    GSH = II // NCORES
    in_maps = []
    for c in range(NCORES):
        cos_t, sin_t = _rope_tables(T, c)
        in_maps.append({
            "x_in": np.ascontiguousarray(X[0, c * T:(c + 1) * T, :], dtype=np.float32),
            "winT": winT, "binx": binx, "woutT": woutT, "boutx": boutx,
            "gsh": np.ascontiguousarray(gw[c * GSH:(c + 1) * GSH, :], dtype=np.float32),
            "ush": np.ascontiguousarray(uw[c * GSH:(c + 1) * GSH, :], dtype=np.float32),
            "dsh": np.ascontiguousarray(dw[:, c * GSH:(c + 1) * GSH], dtype=np.float32),
            "cosx": cos_t, "sinx": sin_t, "identx": ident,
        })
    return in_maps


def kernel(**inputs):
    T = SEQ // NCORES
    in_maps = _stage(inputs, T)
    nc = build_nc(T)
    res = run_bass_kernel_spmd(nc, in_maps, core_ids=list(range(NCORES)))
    outp = np.concatenate([np.asarray(r["out"]) for r in res.results], axis=0)
    return outp.reshape(1, SEQ, D).astype(np.float32)
